# version 14
# baseline (speedup 1.0000x reference)
import numpy as np
import ml_dtypes
import concourse.bass as bass
import concourse.mybir as mybir
from concourse.bass_utils import run_bass_kernel_spmd

F32 = mybir.dt.float32
BF16 = mybir.dt.bfloat16
I32 = mybir.dt.int32
ALU = mybir.AluOpType
AF = mybir.ActivationFunctionType
AXL = mybir.AxisListType

B, L, E, H, D = 4, 8192, 1024, 16, 64
NCORES = 8
TPC = L * B // NCORES
NCH = TPC // 128
NSC = TPC // 512
ECH = E // 128
INNER = 4096
NSTR = INNER // 512
ICH = INNER // 128
NPAIR = 8
EPS_ATTN = 1e-6
EPS_LN = 1e-5
WCNT = float(INNER * E)

ENGINES = ("sync", "scalar", "vector", "tensor", "gpsimd")
DBG = False


class Prog:

    def __init__(self, nc):
        self.nc = nc
        self.q = {e: [] for e in ENGINES}
        self.sems = {}
        self.plan = {}
        self.wr = {e: set() for e in ENGINES}
        self.rd = {e: set() for e in ENGINES}
        self.dmak = {}

    def sem(self, name):
        h = self.nc.semaphore(name).__enter__()
        self.sems[name] = h
        self.plan[name] = 0

    @staticmethod
    def _names(aps):
        out = set()
        for a in aps:
            t = getattr(a, "tensor", None)
            if t is not None:
                out.add(t.name)
        return out

    def wait(self, eng, sem, value=None):
        if value is None:
            value = self.plan[sem]
        if value <= 0:
            return
        h = self.sems[sem]
        self.q[eng].append(lambda e, h=h, v=value: e.wait_ge(h, v))

    def op(self, eng, method, outs, ins, sem=None, **kw):
        onames = self._names(outs)
        inames = self._names(ins)
        if (inames & self.wr[eng]) or (onames & (self.wr[eng] | self.rd[eng])):
            self.q[eng].append(lambda e: e.drain())
            self.wr[eng].clear()
            self.rd[eng].clear()
        self.wr[eng] |= onames
        self.rd[eng] |= inames
        semh = self.sems[sem] if sem else None
        if sem:
            self.plan[sem] += 1

        def f(e, method=method, kw=kw, semh=semh):
            inst = getattr(e, method)(**kw)
            if semh is not None:
                inst.then_inc(semh, 1)
        self.q[eng].append(f)

    def dma(self, eng, out, in_, sem):
        semh = self.sems[sem]
        self.plan[sem] += 16

        def f(e, out=out, in_=in_, semh=semh):
            e.dma_start(out=out, in_=in_).then_inc(semh, 16)
        self.q[eng].append(f)

    def pdma(self, eng, stream, out, in_):
        k = self.dmak.get(stream, 0)
        self.dmak[stream] = k + 1
        sem = f"{stream}{k % 2}"
        if k >= 2:
            self.wait(eng, sem, 16 * (k // 2))
        self.dma(eng, out, in_, sem)
        return (sem, 16 * (k // 2 + 1))

    def raw(self, eng, fn):
        self.q[eng].append(fn)

    def barrier(self, name="bar"):
        h = self.sems[name]
        self.plan[name] += 5
        v = self.plan[name]
        for eng in ENGINES:
            def f(e, h=h):
                e.drain()
                e.nop().then_inc(h, 1)
            self.q[eng].append(f)
            self.wr[eng].clear()
            self.rd[eng].clear()
        for eng in ENGINES:
            self.q[eng].append(lambda e, h=h, v=v: e.wait_ge(h, v))


def build_full():
    nc = bass.Bass("TRN2", target_bir_lowering=False, debug=False)
    p = Prog(nc)
    W, O, DMA = p.wait, p.op, p.dma

    ap = lambda t: t.ap()
    xT = ap(nc.dram_tensor("xT", [E, TPC], F32, kind="ExternalInput"))
    x_tm = ap(nc.dram_tensor("x", [TPC, E], F32, kind="ExternalInput"))
    wqT = ap(nc.dram_tensor("wqT", [E, E], F32, kind="ExternalInput"))
    wkT = ap(nc.dram_tensor("wkT", [E, E], F32, kind="ExternalInput"))
    wvT = ap(nc.dram_tensor("wvT", [E, E], F32, kind="ExternalInput"))
    woT = ap(nc.dram_tensor("woT", [E, E], F32, kind="ExternalInput"))
    bqpp = ap(nc.dram_tensor("bqpp", [128, ECH], F32, kind="ExternalInput"))
    bkb_d = ap(nc.dram_tensor("bkb", [128, E], F32, kind="ExternalInput"))
    bob_d = ap(nc.dram_tensor("bob", [128, E], F32, kind="ExternalInput"))
    bv2_d = ap(nc.dram_tensor("bv2", [2, 512], F32, kind="ExternalInput"))
    wgTd = ap(nc.dram_tensor("wgT", [E, INNER], F32, kind="ExternalInput"))
    wuTd = ap(nc.dram_tensor("wuT", [E, INNER], F32, kind="ExternalInput"))
    wdTd = ap(nc.dram_tensor("wdT", [INNER, E], F32, kind="ExternalInput"))
    eyeb_d = ap(nc.dram_tensor("eyeb", [128, 128], BF16, kind="ExternalInput"))
    eyef_d = ap(nc.dram_tensor("eyef", [128, 128], F32, kind="ExternalInput"))
    out_d = ap(nc.dram_tensor("out", [TPC, E], F32, kind="ExternalOutput"))

    if DBG:
        dbg_r1 = ap(nc.dram_tensor("dbg_r1", [TPC, E], F32,
                                   kind="ExternalOutput"))
        dbg_sf = ap(nc.dram_tensor("dbg_sf", [128, NPAIR * 132], F32,
                                   kind="ExternalOutput"))
        dbg_mh = ap(nc.dram_tensor("dbg_mh", [128, 3], F32,
                                   kind="ExternalOutput"))
        dbg_hq = ap(nc.dram_tensor("dbg_hq", [128, INNER], BF16,
                                   kind="ExternalOutput"))
    cc_in = nc.dram_tensor("cc_in", [128, NPAIR * 132], F32)
    cc_out = nc.dram_tensor("cc_out", [128, NPAIR * 132], F32)
    r1h = ap(nc.dram_tensor("r1h", [TPC, E], F32))
    wgt_h = ap(nc.dram_tensor("wgth", [E, INNER], BF16))
    wut_h = ap(nc.dram_tensor("wuth", [E, INNER], BF16))
    hqTh = ap(nc.dram_tensor("hqTh", [NCH, 128, INNER], BF16))

    live = {}

    def alloc(name, shape, dtype, side="left", psum=False):
        if psum:
            cm = nc.psum_tensor(name, shape, dtype)
        else:
            cm = nc.sbuf_tensor(name, shape, dtype, side=side)
        t = cm.__enter__()
        live[name] = cm
        return t

    def free(*names):
        for n in names:
            live.pop(n).__exit__(None, None, None)

    for s in ("qconst", "qwa", "qcc", "qccr", "ccd",
              "qxt0", "qxt1", "qws0", "qws1", "qx0", "qx1", "qr1w0", "qr1w1",
              "qr1r0", "qr1r1", "qwf0", "qwf1", "qhw0", "qhw1", "qhr0",
              "qtw0", "qtw1", "qwl0", "qwl1",
              "qhr1", "qo0", "qo1",
              "sKV", "sPhiD", "sPhiK", "sER", "sVc", "sS", "sWred",
              "sAf", "sQmm", "sER2", "sPhiQ", "sSb",
              "sNum", "sDiv", "sCtxT", "sCtxE", "sWo", "sXpb", "sR0", "sR1",
              "sTg", "sTu", "sTd",
              "sXq", "sXqT", "sXqE", "sGU", "sSilu", "sH", "sHq", "sHqT",
              "sHqE", "sDn", "sEp", "sF1f", "sRr", "sCd", "sTa", "bar"):
        p.sem(s)

    eyeb = alloc("eyeb_s", [128, 128], BF16)
    eyef = alloc("eyef_s", [128, 128], F32)
    bq_s = alloc("bq_s", [128, ECH], F32)
    onesb = alloc("onesb", [128, 1], BF16)
    onesr = alloc("onesr", [1, 128], F32)
    zcol = alloc("zcol", [128, 1], F32)
    onecol = alloc("onecol", [128, 1], F32)
    mhb = alloc("mhb", [128, 3], F32)
    thb = alloc("thb", [128, 3], F32)
    nthb = alloc("nthb", [128, 3], F32)
    mh127 = alloc("mh127", [128, 3], F32)
    mxh_all = alloc("mxh_all", [128, NCH], F32)
    tcol = alloc("tcol", [128, 16], F32)
    fcol = alloc("fcol", [128, 16], F32)
    r0b = alloc("r0b", [128, E], F32)
    r1b = alloc("r1b", [128, 2 * E], F32)

    class PV:
        def __init__(self, a):
            self._a = a

        def ap(self):
            return self._a

        def __getitem__(self, k):
            return self._a[k]

    pbig = alloc("pbig", [128, 4096], F32, psum=True)
    pb = [PV(pbig[:, 512 * k:512 * (k + 1)]) for k in range(8)]
    pbbig = pbig.ap().bitcast(BF16)
    pbb = [pbbig[:, 1024 * k:1024 * (k + 1)] for k in range(8)]

    DMA("sync", eyeb[:], eyeb_d, "qconst")
    DMA("sync", eyef[:], eyef_d, "qconst")
    DMA("sync", bq_s[:], bqpp, "qconst")
    O("vector", "memset", [onesb[:]], [], sem="sSb", ap=onesb[:], constant=1.0)
    O("vector", "memset", [onesr[:]], [], sem="sSb", ap=onesr[:], constant=1.0)
    O("vector", "memset", [zcol[:]], [], sem="sSb", ap=zcol[:], constant=0.0)
    O("vector", "memset", [onecol[:]], [], sem="sSb", ap=onecol[:], constant=1.0)

    bkb = alloc("bkb_s", [128, E], F32)
    bob = alloc("bob_s", [128, E], F32)
    bv2 = alloc("bv2_s", [2, 512], F32)
    ccs = alloc("ccs", [128, NPAIR * 132], F32)
    sfull = alloc("sfullb", [128, NPAIR * 130], BF16)
    wparts = alloc("wparts", [128, 48], F32)
    xqbuf = alloc("xqbuf", [128, ECH * TPC], BF16)

    phik = alloc("phik", [128, 2 * E], BF16, side="right")
    vbuf = alloc("vbuf", [128, 2 * E], BF16, side="right")
    kqbuf = alloc("kqbuf", [128, E], F32, side="right")
    ebuf = alloc("ebuf", [128, E], F32, side="right")
    rbuf = alloc("rbuf", [128, E], F32, side="right")
    wk_s = alloc("wk_s", [128, ECH * E], BF16, side="right")
    wv_s = alloc("wv_s", [128, ECH * E], BF16, side="right")
    wstage = alloc("wstage", [128, 2 * 2048], F32, side="right")

    DMA("sync", bkb[:], bkb_d, "qconst")
    DMA("sync", bob[:], bob_d, "qconst")
    DMA("sync", bv2[:], bv2_d, "qconst")
    O("vector", "memset", [sfull[:]], [], sem="sSb", ap=sfull[:], constant=0.0)
    O("vector", "memset", [ccs[:]], [], sem="sSb", ap=ccs[:], constant=0.0)
    for k in range(3):
        O("vector", "memset", [pb[k].ap()], [], sem="sSb",
          ap=pb[k][:, :], constant=0.0)
    for _e in ("tensor", "vector", "scalar", "gpsimd"):
        W(_e, "qconst", p.plan["qconst"])
        W(_e, "sSb", p.plan["sSb"])

    for wsrc, wdst in ((wkT, wk_s), (wvT, wv_s)):
        DMA("gpsimd", wdst[:].rearrange("p (c f) -> p c f", c=ECH),
            wsrc.rearrange("(c p) f -> p c f", p=128), "qwa")
    xr = xT.rearrange("(c p) t -> p c t", p=128)
    xq3 = xqbuf[:].rearrange("p (c t) -> p c t", c=ECH)
    xt_ck = []
    for sc in range(NSC):
        xt_ck.append(p.pdma("gpsimd", "qxt",
                            xq3[:, :, sc * 512:(sc + 1) * 512],
                            xr[:, :, sc * 512:(sc + 1) * 512]))

    wpieces = []
    for wd_ in (wgTd, wuTd, wdTd):
        nrowch = wd_.shape[0] // 128
        ncol = wd_.shape[1]
        r = wd_.rearrange("(c p) f -> p c f", p=128)
        if ncol >= 2048:
            for i in range(nrowch):
                for j in range(ncol // 2048):
                    wpieces.append(r[:, i:i + 1, 2048 * j:2048 * (j + 1)])
        else:
            per = 2048 // ncol
            for i in range(0, nrowch, per):
                wpieces.append(r[:, i:i + per, :])
    assert len(wpieces) == 48
    ws_ck = []

    wk3 = wk_s[:].rearrange("p (c f) -> p c f", c=ECH)
    wv3 = wv_s[:].rearrange("p (c f) -> p c f", c=ECH)
    sbank = [0, 0, 0, 1, 1, 1, 2, 2]
    scol = [0, 132, 264, 0, 132, 264, 0, 132]

    def emit_s_mms(tc):
        par = tc % 2
        W("tensor", "sPhiK")
        W("tensor", "sVc")
        for pr in range(NPAIR):
            bk_, co = sbank[pr], scol[pr]
            lt = phik[:, par * E + 128 * pr: par * E + 128 * (pr + 1)]
            O("tensor", "ldweights", [], [], weights=lt)
            O("tensor", "matmul", [pb[bk_].ap()], [],
              out=pb[bk_][:, co:co + 128], lhsT=lt,
              rhs=vbuf[:, par * E + 128 * pr: par * E + 128 * (pr + 1)],
              start=(tc == 0 and pr % 3 == 0), stop=False,
              skip_group_check=True)
            O("tensor", "matmul", [pb[bk_].ap()], [],
              sem=("sS" if pr == NPAIR - 1 else None),
              out=pb[bk_][:, co + 128:co + 129], lhsT=lt, rhs=onesb[:],
              start=False, stop=(tc == NCH - 1), skip_group_check=True)

    for tc in range(NCH):
        par = tc % 2
        sc = tc // 4
        W("tensor", *xt_ck[sc])
        if tc > 0:
            W("tensor", "sPhiD", tc)
            W("tensor", "sVc", tc)
        if tc == 0:
            W("tensor", "qwa", 32)
        for e in range(ECH):
            lt = xq3[:, e, tc * 128:(tc + 1) * 128]
            O("tensor", "ldweights", [], [], weights=lt)
            for half, bnk, w3 in ((0, 3, wk3), (1, 4, wk3),
                                  (0, 5, wv3), (1, 6, wv3)):
                O("tensor", "matmul", [pb[bnk].ap()], [],
                  sem=("sKV" if (e == ECH - 1 and bnk == 6) else None),
                  out=pb[bnk][:, :], lhsT=lt,
                  rhs=w3[:, e, half * 512:(half + 1) * 512],
                  start=(e == 0), stop=(e == ECH - 1))
        if tc > 0:
            emit_s_mms(tc - 1)

        if tc >= 1:
            W("vector", "sER")
        W("vector", "sKV", tc + 1)
        O("vector", "tensor_tensor", [kqbuf[:]], [pb[3].ap(), bkb[:]],
          out=kqbuf[:, 0:512], in0=pb[3][:, :], in1=bkb[:, 0:512], op=ALU.add)
        O("vector", "tensor_tensor", [kqbuf[:]], [pb[4].ap(), bkb[:]],
          sem="sPhiD",
          out=kqbuf[:, 512:1024], in0=pb[4][:, :], in1=bkb[:, 512:1024],
          op=ALU.add)
        if tc >= 1:
            W("scalar", "sPhiK")
        W("scalar", "sPhiD", p.plan["sPhiD"])
        O("scalar", "activation", [ebuf[:]], [kqbuf[:]],
          out=ebuf[:], in_=kqbuf[:], func=AF.Exp)
        O("scalar", "activation", [rbuf[:]], [kqbuf[:]], sem="sER",
          out=rbuf[:], in_=kqbuf[:], func=AF.Relu)
        if tc >= 2:
            W("vector", "sS", tc - 1)
        W("vector", "sER", p.plan["sER"])
        O("vector", "scalar_tensor_tensor", [phik[:]], [ebuf[:], rbuf[:]],
          sem="sPhiK",
          out=phik[:, par * E:(par + 1) * E], in0=ebuf[:], scalar=1.0,
          in1=rbuf[:], op0=ALU.min, op1=ALU.add)
        W("scalar", "sKV", p.plan["sKV"])
        if tc >= 2:
            W("scalar", "sS", tc - 1)
        O("scalar", "copy", [vbuf[:]], [pb[5].ap()],
          out=vbuf[:, par * E:par * E + 512], in_=pb[5][:, :])
        O("scalar", "copy", [vbuf[:]], [pb[6].ap()], sem="sVc",
          out=vbuf[:, par * E + 512:(par + 1) * E], in_=pb[6][:, :])

        idxs = [i for i in range(48) if i * 32 // 48 == tc]
        for idx in idxs:
            bufp = idx % 2
            if idx >= 2:
                W("sync", "sWred", idx - 1)
            ws_ck.append(p.pdma("sync", "qws",
                                wstage[:, bufp * 2048:(bufp + 1) * 2048],
                                wpieces[idx]))
            W("vector", *ws_ck[idx])
            O("vector", "tensor_reduce", [wparts[:]], [wstage[:]], sem="sWred",
              out=wparts[:, idx:idx + 1],
              in_=wstage[:, bufp * 2048:(bufp + 1) * 2048],
              axis=AXL.X, op=ALU.add, apply_absolute_value=True)

    emit_s_mms(NCH - 1)

    W("vector", "sWred", p.plan["sWred"])
    for w in range(3):
        O("vector", "tensor_reduce", [tcol[:]], [wparts[:]],
          sem=("sAf" if w == 2 else None),
          out=tcol[:, w:w + 1], in_=wparts[:, 16 * w:16 * (w + 1)],
          axis=AXL.X, op=ALU.add)
    W("tensor", "sAf", p.plan["sAf"])
    W("tensor", "sS", p.plan["sS"])
    O("tensor", "transpose", [pb[7].ap()], [tcol[:], eyef[:]], sem="sAf",
      out=pb[7][0:3, 0:128], in_=tcol[:, 0:3], identity=eyef[:])
    W("vector", "sAf", p.plan["sAf"])
    O("vector", "tensor_reduce", [tcol[:]], [pb[7].ap()],
      out=tcol[0:3, 3:4], in_=pb[7][0:3, 0:128], axis=AXL.X, op=ALU.add)
    O("vector", "tensor_scalar", [tcol[:]], [tcol[:]], sem="sAf",
      out=tcol[0:3, 4:5], in0=tcol[0:3, 3:4], scalar1=1.0 / WCNT,
      scalar2=1e-5, op0=ALU.mult, op1=ALU.max)
    W("tensor", "sAf", p.plan["sAf"])
    O("tensor", "transpose", [pb[6].ap()], [tcol[:], eyef[:]], sem="sAf",
      out=pb[6][0:1, 0:3], in_=tcol[0:3, 4:5], identity=eyef[0:3, 0:3])
    W("vector", "sAf", p.plan["sAf"])
    O("vector", "tensor_copy", [tcol[:]], [pb[6].ap()], sem="sAf",
      out=tcol[0:1, 5:8], in_=pb[6][0:1, 0:3])
    W("tensor", "sAf", p.plan["sAf"])
    O("tensor", "matmul", [pb[5].ap()], [onesr[:], tcol[:]], sem="sAf",
      out=pb[5][:, 0:3], lhsT=onesr[:], rhs=tcol[0:1, 5:8],
      start=True, stop=True)
    W("vector", "sAf", p.plan["sAf"])
    O("vector", "tensor_copy", [mhb[:]], [pb[5].ap()],
      out=mhb[:], in_=pb[5][:, 0:3])
    O("vector", "tensor_scalar", [thb[:]], [mhb[:]],
      out=thb[:], in0=mhb[:], scalar1=0.5, scalar2=None, op0=ALU.mult)
    O("vector", "tensor_scalar", [nthb[:]], [mhb[:]],
      out=nthb[:], in0=mhb[:], scalar1=-0.5, scalar2=None, op0=ALU.mult)
    O("vector", "tensor_scalar", [mh127[:]], [mhb[:]], sem="sAf",
      out=mh127[:], in0=mhb[:], scalar1=1.0 / 254.0, scalar2=None,
      op0=ALU.mult)

    W("vector", "sS", p.plan["sS"])
    for k in range(3):
        ncols = 396 if k < 2 else 264
        O("vector", "tensor_copy", [ccs[:]], [pb[k].ap()],
          sem=("sSb" if k == 2 else None),
          out=ccs[:, 396 * k:396 * k + ncols], in_=pb[k][:, 0:ncols])
    W("sync", "sSb", p.plan["sSb"])
    DMA("sync", cc_in.ap(), ccs[:], "qcc")
    free("wstage", "wv_s", "wk_s")
    phiqt = alloc("phiqt", [128, ECH * TPC], BF16)
    wq_s = alloc("wq_s", [128, ECH * E], BF16, side="right")
    wq3 = wq_s[:].rearrange("p (c f) -> p c f", c=ECH)
    W("gpsimd", "qcc", p.plan["qcc"])

    def _cc(e):
        e.collective_compute(
            "AllReduce", ALU.add,
            replica_groups=[[0, 1], [2, 3], [4, 5], [6, 7]],
            ins=[cc_in.ap().opt()], outs=[cc_out.ap().opt()],
        ).then_inc(p.sems["ccd"], 1)
    p.raw("gpsimd", _cc)
    p.plan["ccd"] += 1
    W("gpsimd", "sS", p.plan["sS"])
    W("gpsimd", "sVc", p.plan["sVc"])
    DMA("gpsimd", wq_s[:].rearrange("p (c f) -> p c f", c=ECH),
        wqT.rearrange("(c p) f -> p c f", p=128), "qwa")
    W("gpsimd", "ccd", p.plan["ccd"])
    DMA("gpsimd", ccs[:], cc_out.ap(), "qccr")

    pq3 = phiqt[:].rearrange("p (c t) -> p c t", c=ECH)

    qmm_ck, phiq_ck, er2_ck = [], [], []
    W("tensor", "qwa", p.plan["qwa"])
    W("tensor", "sSb", p.plan["sSb"])
    W("tensor", "sAf", p.plan["sAf"])
    for sc in range(NSC):
        for f in range(ECH):
            i = sc * ECH + f
            if sc > 0:
                W("tensor", "sPhiQ", phiq_ck[(sc - 1) * ECH + f])
            for e in range(ECH):
                lt = wq3[:, e, 128 * f:128 * (f + 1)]
                O("tensor", "ldweights", [], [], weights=lt)
                O("tensor", "matmul", [pb[f].ap()], [],
                  sem=("sQmm" if e == ECH - 1 else None),
                  out=pb[f][:, :], lhsT=lt,
                  rhs=xq3[:, e, sc * 512:(sc + 1) * 512],
                  start=(e == 0), stop=(e == ECH - 1))
            qmm_ck.append(p.plan["sQmm"])
            par = f % 2
            if i >= 2:
                W("scalar", "sPhiQ", phiq_ck[i - 2])
            W("scalar", "sQmm", qmm_ck[i])
            O("scalar", "activation", [ebuf[:]], [pb[f].ap(), bq_s[:], onecol[:]],
              out=ebuf[:, par * 512:(par + 1) * 512], in_=pb[f][:, :],
              func=AF.Exp, bias=bq_s[:, f:f + 1], scale=onecol[:, 0:1])
            O("scalar", "activation", [rbuf[:]], [pb[f].ap(), bq_s[:], onecol[:]],
              sem="sER2",
              out=rbuf[:, par * 512:(par + 1) * 512], in_=pb[f][:, :],
              func=AF.Relu, bias=bq_s[:, f:f + 1], scale=onecol[:, 0:1])
            er2_ck.append(p.plan["sER2"])
            W("vector", "sER2", er2_ck[i])
            O("vector", "scalar_tensor_tensor", [pq3], [ebuf[:], rbuf[:]],
              sem="sPhiQ",
              out=pq3[:, f, sc * 512:(sc + 1) * 512],
              in0=ebuf[:, par * 512:(par + 1) * 512], scalar=1.0,
              in1=rbuf[:, par * 512:(par + 1) * 512],
              op0=ALU.min, op1=ALU.add)
            phiq_ck.append(p.plan["sPhiQ"])

    zb2a = alloc("zb2a", [128, 16], F32, side="right")
    zall = alloc("zall", [2, 1024], F32, side="right")
    W("vector", "qccr", p.plan["qccr"])
    W("tensor", "sPhiQ", p.plan["sPhiQ"])
    O("vector", "memset", [zb2a[:]], [], ap=zb2a[:], constant=0.0)
    for pr in range(NPAIR):
        base = 132 * pr
        O("vector", "tensor_copy", [zb2a[:]], [ccs[:]],
          out=zb2a[0:64, 2 * pr:2 * pr + 1],
          in_=ccs[0:64, base + 128:base + 129])
        O("vector", "tensor_copy", [zb2a[:]], [ccs[:]],
          sem=("sSb" if pr == NPAIR - 1 else None),
          out=zb2a[64:128, 2 * pr + 1:2 * pr + 2],
          in_=ccs[64:128, base + 128:base + 129])
    W("tensor", "sSb", p.plan["sSb"])
    for pr in range(NPAIR):
        bnk = 7 if pr < 4 else 3
        O("tensor", "transpose", [pb[bnk].ap()], [zb2a[:], eyef[:]],
          sem=("sSb" if pr % 4 == 3 else None),
          out=pb[bnk][0:2, 128 * (pr % 4):128 * (pr % 4) + 128],
          in_=zb2a[:, 2 * pr:2 * pr + 2], identity=eyef[:])
    W("vector", "sSb", p.plan["sSb"])
    O("vector", "tensor_copy", [zall[:]], [pb[7].ap()],
      out=zall[:, 0:512], in_=pb[7][0:2, 0:512])
    O("vector", "tensor_copy", [zall[:]], [pb[3].ap()], sem="sSb",
      out=zall[:, 512:1024], in_=pb[3][0:2, 0:512])
    W("tensor", "sSb", p.plan["sSb"])
    for pr in range(NPAIR):
        O("tensor", "matmul", [pb[6].ap()], [zall[:], bv2[:]],
          sem=("sSb" if pr == NPAIR - 1 else None),
          out=pb[6][:, 64 * pr:64 * (pr + 1)],
          lhsT=zall[:, 128 * pr:128 * pr + 128],
          rhs=bv2[:, 64 * pr:64 * (pr + 1)],
          start=(pr == 0), stop=(pr == NPAIR - 1), skip_group_check=True)
    W("vector", "sSb", p.plan["sSb"])
    for pr in range(NPAIR):
        base = 132 * pr
        sb_ = 130 * pr
        O("vector", "tensor_tensor", [sfull[:]], [ccs[:], pb[6].ap()],
          out=sfull[0:64, sb_:sb_ + 64], in0=ccs[0:64, base:base + 64],
          in1=pb[6][0:64, 64 * pr:64 * pr + 64], op=ALU.add)
        O("vector", "tensor_tensor", [sfull[:]], [ccs[:], pb[6].ap()],
          out=sfull[64:128, sb_ + 64:sb_ + 128],
          in0=ccs[64:128, base + 64:base + 128],
          in1=pb[6][64:128, 64 * pr:64 * pr + 64], op=ALU.add)
        O("vector", "tensor_copy", [sfull[:]], [ccs[:]],
          out=sfull[0:64, sb_ + 128:sb_ + 129],
          in_=ccs[0:64, base + 128:base + 129])
        O("vector", "tensor_copy", [sfull[:]], [ccs[:]],
          sem=("sSb" if pr == NPAIR - 1 else None),
          out=sfull[64:128, sb_ + 129:sb_ + 130],
          in_=ccs[64:128, base + 128:base + 129])
    W("tensor", "sSb", p.plan["sSb"])
    free("zall", "zb2a")

    p.barrier()

    free("wq_s", "rbuf", "ebuf", "kqbuf", "vbuf", "phik")
    wo_s = alloc("wo_s", [128, ECH * E], BF16, side="right")
    xch = alloc("xch", [128, 2 * E], F32, side="right")
    xpb = alloc("xpb", [128, 2 * E], F32, side="right")
    ctx = alloc("ctx", [128, 2 * E], BF16, side="right")
    ctxt = alloc("ctxt", [128, 2 * E], BF16, side="right")
    dgb = alloc("dgb", [128, 32], F32, side="right")
    rden = alloc("rden", [128, 32], F32, side="right")
    r0b2 = alloc("r0b2", [128, E], F32, side="right")
    r0p = [r0b[:], r0b2[:]]
    DMA("gpsimd", wo_s[:].rearrange("p (c f) -> p c f", c=ECH),
        woT.rearrange("(c p) f -> p c f", p=128), "qwa")
    wo3 = wo_s[:].rearrange("p (c f) -> p c f", c=ECH)
    sf3 = sfull[:].rearrange("p (r c) -> p r c", r=NPAIR)
    n_ln = float(E)

    num_ck, div_ck, ctxt_ck, ctxe_ck, wo_ck = [], [], [], [], []
    r0_ck, r1_ck, xpb_ck, sq_ck, qx_ck, r1w_ck = [], [], [], [], [], []

    W("tensor", "qwa", p.plan["qwa"])
    for c in range(NCH):
        par = c % 2
        tc8 = 8 * par
        xpc = xpb[:, par * E:(par + 1) * E]
        r0c = r0p[par]
        ctxc = ctx[:, par * E:(par + 1) * E]
        ctxtc = ctxt[:, par * E:(par + 1) * E]
        if c >= 2:
            W("sync", "sXpb", xpb_ck[c - 2])
        qx_ck.append(p.pdma("sync", "qx", xch[:, par * E:(par + 1) * E],
                            x_tm[c * 128:(c + 1) * 128, :]))
        if c > 0:
            W("tensor", "sDiv", div_ck[c - 1])
        for pr in range(NPAIR):
            bk_, co = pr // 2, 256 * (pr % 2)
            lt = pq3[:, pr, c * 128:(c + 1) * 128]
            O("tensor", "ldweights", [], [], weights=lt)
            O("tensor", "matmul", [pb[bk_].ap()], [sfull[:]],
              sem=("sNum" if pr == NPAIR - 1 else None),
              out=pb[bk_][:, co:co + 130], lhsT=lt, rhs=sf3[:, pr, :],
              start=(pr % 2 == 0), stop=True, skip_group_check=True)
        num_ck.append(p.plan["sNum"])
        W("vector", "sNum", num_ck[c])
        numv = pbig.ap()[:, 0:2048].rearrange("p (r f) -> p r f", r=8)
        O("vector", "tensor_scalar", [dgb[:]], [pbig.ap()],
          out=dgb[:, 16 * par:16 * par + 16].rearrange(
              "p (r a) -> p r a", r=8),
          in0=numv[:, :, 128:130], scalar1=EPS_ATTN, scalar2=None,
          op0=ALU.add)
        O("vector", "reciprocal", [rden[:]], [dgb[:]],
          out=rden[:, 16 * par:16 * par + 16],
          in_=dgb[:, 16 * par:16 * par + 16])
        if c >= 2:
            W("vector", "sCtxT", ctxt_ck[c - 2])
        O("vector", "tensor_tensor", [ctx[:]], [pbig.ap(), rden[:]],
          sem="sDiv",
          out=ctxc.rearrange("p (r a b) -> p r a b", r=8, a=2),
          in0=numv[:, :, 0:128].rearrange("p r (a b) -> p r a b", a=2),
          in1=rden[:, 16 * par:16 * par + 16].rearrange(
              "p (r a) -> p r a", r=8).broadcast_to([128, 8, 2, 64]),
          op=ALU.mult)
        div_ck.append(p.plan["sDiv"])
        W("tensor", "sDiv", div_ck[c])
        if c > 0:
            W("tensor", "sCtxE", ctxe_ck[c - 1])
        for j in range(ECH):
            O("tensor", "transpose", [pbb[4]], [ctx[:], eyeb[:]],
              sem=("sCtxT" if j == ECH - 1 else None),
              out=pbb[4][:, 128 * j:128 * (j + 1)],
              in_=ctxc[:, 128 * j:128 * (j + 1)], identity=eyeb[:])
        ctxt_ck.append(p.plan["sCtxT"])
        W("scalar", "sCtxT", ctxt_ck[c])
        if c >= 2:
            W("scalar", "sWo", wo_ck[c - 2])
        O("scalar", "copy", [ctxt[:]], [pbb[4]], sem="sCtxE",
          out=ctxtc, in_=pbb[4][:, :])
        ctxe_ck.append(p.plan["sCtxE"])
        W("tensor", "sCtxE", ctxe_ck[c])
        if c > 0:
            W("tensor", "sR0", r0_ck[c - 1])
        for e in range(ECH):
            lt = ctxtc[:, 128 * e:128 * (e + 1)]
            O("tensor", "ldweights", [], [], weights=lt)
            for half, bnk in ((0, 5), (1, 6)):
                O("tensor", "matmul", [pb[bnk].ap()], [ctxt[:]],
                  sem=("sWo" if (e == ECH - 1 and half == 1) else None),
                  out=pb[bnk][:, :], lhsT=lt,
                  rhs=wo3[:, e, half * 512:(half + 1) * 512],
                  start=(e == 0), stop=(e == ECH - 1))
        wo_ck.append(p.plan["sWo"])
        W("gpsimd", *qx_ck[c])
        if c >= 2:
            W("gpsimd", "sR1", sq_ck[c - 2])
        O("gpsimd", "tensor_tensor", [xpb[:]], [xch[:], bob[:]], sem="sXpb",
          out=xpc, in0=xch[:, par * E:(par + 1) * E], in1=bob[:],
          op=ALU.add)
        xpb_ck.append(p.plan["sXpb"])
        W("vector", "sWo", wo_ck[c])
        W("vector", "sXpb", xpb_ck[c])
        if c >= 2:
            W("vector", "sR1", r1_ck[c - 2])
        O("vector", "memset", [tcol[:]], [], ap=tcol[:, tc8:tc8 + 2],
          constant=0.0)
        O("vector", "memset", [tcol[:]], [], ap=tcol[:, tc8 + 3:tc8 + 4],
          constant=0.0)
        O("vector", "scalar_tensor_tensor", [r0c, tcol[:]],
          [pb[5].ap(), xpb[:]],
          out=r0c[:, 0:512], in0=pb[5][:, :], scalar=1.0, in1=xpc[:, 0:512],
          op0=ALU.mult, op1=ALU.add, accum_out=tcol[:, tc8:tc8 + 1])
        O("vector", "scalar_tensor_tensor", [r0c, tcol[:]],
          [pb[6].ap(), xpb[:]], sem="sR0",
          out=r0c[:, 512:1024], in0=pb[6][:, :], scalar=1.0,
          in1=xpc[:, 512:1024], op0=ALU.mult, op1=ALU.add,
          accum_out=tcol[:, tc8 + 1:tc8 + 2])
        r0_ck.append(p.plan["sR0"])
        O("vector", "tensor_scalar", [tcol[:]], [tcol[:]],
          out=tcol[:, tc8 + 2:tc8 + 3], in0=tcol[:, tc8:tc8 + 1],
          scalar1=tcol[:, tc8 + 1:tc8 + 2], scalar2=None, op0=ALU.add)
        W("scalar", "sR0", r0_ck[c])
        O("scalar", "activation", [xpb[:], tcol[:]], [r0c], sem="sR1",
          out=xpc, in_=r0c, func=AF.Square, accum_out=tcol[:, tc8 + 3:tc8 + 4])
        sq_ck.append(p.plan["sR1"])
        W("vector", "sR1", p.plan["sR1"])
        O("vector", "tensor_scalar", [tcol[:]], [tcol[:]],
          out=tcol[:, tc8 + 4:tc8 + 5], in0=tcol[:, tc8 + 2:tc8 + 3],
          scalar1=tcol[:, tc8 + 2:tc8 + 3], scalar2=-1.0 / (n_ln * n_ln),
          op0=ALU.mult, op1=ALU.mult)
        O("vector", "tensor_scalar", [tcol[:]], [tcol[:]],
          out=tcol[:, tc8 + 4:tc8 + 5], in0=tcol[:, tc8 + 4:tc8 + 5],
          scalar1=EPS_LN, scalar2=None, op0=ALU.add)
        O("vector", "tensor_scalar", [tcol[:]], [tcol[:]], sem="sR1",
          out=tcol[:, tc8 + 5:tc8 + 6], in0=tcol[:, tc8 + 3:tc8 + 4],
          scalar1=1.0 / n_ln, scalar2=tcol[:, tc8 + 4:tc8 + 5],
          op0=ALU.mult, op1=ALU.add)
        W("scalar", "sR1", p.plan["sR1"])
        O("scalar", "activation", [tcol[:]], [tcol[:]], sem="sR1",
          out=tcol[:, tc8 + 6:tc8 + 7], in_=tcol[:, tc8 + 5:tc8 + 6],
          func=AF.Sqrt)
        W("vector", "sR1", p.plan["sR1"])
        O("vector", "reciprocal", [tcol[:]], [tcol[:]],
          out=tcol[:, tc8 + 7:tc8 + 8], in_=tcol[:, tc8 + 6:tc8 + 7])
        O("vector", "tensor_scalar", [tcol[:]], [tcol[:]], sem="sR1",
          out=tcol[:, tc8:tc8 + 1], in0=tcol[:, tc8 + 2:tc8 + 3],
          scalar1=-1.0 / n_ln, scalar2=tcol[:, tc8 + 7:tc8 + 8],
          op0=ALU.mult, op1=ALU.mult)
        if c >= 2:
            W("vector", *r1w_ck[c - 2])
        O("vector", "tensor_scalar", [r1b[:]], [r0c, tcol[:]], sem="sR1",
          out=r1b[:, par * E:(par + 1) * E], in0=r0c,
          scalar1=tcol[:, tc8 + 7:tc8 + 8], scalar2=tcol[:, tc8:tc8 + 1],
          op0=ALU.mult, op1=ALU.add)
        r1_ck.append(p.plan["sR1"])
        W("scalar", "sR1", r1_ck[c])
        r1w_ck.append(p.pdma("scalar", "qr1w", r1h[c * 128:(c + 1) * 128, :],
                             r1b[:, par * E:(par + 1) * E]))

    p.barrier()

    free("r0b2", "rden", "dgb", "ctxt", "ctx", "xpb", "xch", "wo_s")
    free("phiqt", "xqbuf", "wparts", "sfullb", "ccs", "bv2_s",
         "bob_s", "bkb_s")
    wg_s = alloc("wg_s", [128, ECH * INNER], BF16)
    wu_s = alloc("wu_s", [128, ECH * INNER], BF16)
    r1f = alloc("r1f", [128, 2 * E], F32)
    wfst = alloc("wfst2", [128, 1024], F32)
    tern_a = alloc("tern_a2", [128, 2 * 512], BF16)
    tern_b = alloc("tern_b2", [128, 2 * 512], BF16)
    hbuf = alloc("hbuf", [128, INNER], F32, side="right")
    q32h = alloc("q32h", [128, 2 * 512], I32, side="right")
    f1h = alloc("f1h", [128, 2 * 512], F32, side="right")
    rres = alloc("rres", [128, 2 * 512], F32, side="right")
    hqb = alloc("hqb", [128, 2 * 512], BF16, side="right")
    hqt_st = alloc("hqt_st", [128, 2048], BF16, side="right")
    xqt = alloc("xqt", [128, 2 * E], BF16, side="right")
    xt1 = alloc("xt1", [128, 512], F32, side="right")
    silub = alloc("silub", [128, 1024], F32, side="right")
    hmx8 = alloc("hmx8", [128, 16], F32, side="right")

    wg3 = wg_s[:].rearrange("p (c f) -> p c f", c=ECH)
    wu3 = wu_s[:].rearrange("p (c f) -> p c f", c=ECH)
    fpieces = []
    for wi, wd_ in ((0, wgTd), (1, wuTd)):
        r = wd_.rearrange("(c p) f -> p c f", p=128)
        for n in range(NSTR):
            for e in range(ECH):
                fpieces.append((wi, n, e, r[:, e, 512 * n:512 * (n + 1)]))
    tern_ck, ta_ck, wf_ck = [], [], []
    for idx, (wi, n, e, src_) in enumerate(fpieces):
        wsem = "sTg" if wi == 0 else "sTu"
        par_ = idx % 2
        if idx >= 2:
            W("sync", "sTa", ta_ck[idx - 2])
        stg = wfst[:, par_ * 512:(par_ + 1) * 512]
        wf_ck.append(p.pdma("sync", "qwf", stg, src_))
        W("scalar", *wf_ck[-1])
        if idx >= 2:
            W("scalar", "sTg" if fpieces[idx - 2][0] == 0 else "sTu",
              tern_ck[idx - 2])
        O("scalar", "activation", [tern_a[:]], [wfst[:], nthb[:], onecol[:]],
          out=tern_a[:, par_ * 512:(par_ + 1) * 512], in_=stg, func=AF.Sign,
          bias=nthb[:, wi:wi + 1], scale=onecol[:, 0:1])
        O("scalar", "activation", [tern_b[:]], [wfst[:], thb[:], onecol[:]],
          sem="sTa",
          out=tern_b[:, par_ * 512:(par_ + 1) * 512], in_=stg, func=AF.Sign,
          bias=thb[:, wi:wi + 1], scale=onecol[:, 0:1])
        ta_ck.append(p.plan["sTa"])
        W("vector", "sTa", ta_ck[-1])
        w_sb3 = wg3 if wi == 0 else wu3
        O("vector", "tensor_tensor", [w_sb3], [tern_a[:], tern_b[:]],
          sem=wsem,
          out=w_sb3[:, e, 512 * n:512 * (n + 1)],
          in0=tern_a[:, par_ * 512:(par_ + 1) * 512],
          in1=tern_b[:, par_ * 512:(par_ + 1) * 512], op=ALU.add)
        tern_ck.append(p.plan[wsem])

    rnd_ck = {"sF1f": [], "sHq": []}
    rnd_n = [0]

    def emit_round(src, s_ap, t1_ap):
        k = rnd_n[0]
        rnd_n[0] += 1
        pi = k % 2
        if k >= 2:
            W("scalar", "sHq", rnd_ck["sHq"][k - 2])
        O("scalar", "activation", [q32h[:]], [src, s_ap],
          out=q32h[:, pi * 512:(pi + 1) * 512], in_=src, func=AF.Copy,
          scale=s_ap, bias=0.0)
        O("scalar", "activation", [f1h[:]], [q32h[:]], sem="sF1f",
          out=f1h[:, pi * 512:(pi + 1) * 512],
          in_=q32h[:, pi * 512:(pi + 1) * 512], func=AF.Copy,
          scale=1.0, bias=-1.0)
        rnd_ck["sF1f"].append(p.plan["sF1f"])
        W("vector", "sF1f", rnd_ck["sF1f"][k])
        O("vector", "scalar_tensor_tensor", [rres[:]], [src, s_ap, f1h[:]],
          out=rres[:, pi * 512:(pi + 1) * 512], in0=src, scalar=s_ap,
          in1=f1h[:, pi * 512:(pi + 1) * 512],
          op0=ALU.mult, op1=ALU.subtract)
        O("vector", "scalar_tensor_tensor", [t1_ap], [rres[:], f1h[:]],
          out=t1_ap, in0=rres[:, pi * 512:(pi + 1) * 512], scalar=1.5,
          in1=f1h[:, pi * 512:(pi + 1) * 512], op0=ALU.is_gt, op1=ALU.add)
        O("vector", "scalar_tensor_tensor",
          [hqb[:]], [rres[:], t1_ap], sem="sHq",
          out=hqb[:, pi * 512:(pi + 1) * 512],
          in0=rres[:, pi * 512:(pi + 1) * 512], scalar=0.5,
          in1=t1_ap, op0=ALU.is_ge, op1=ALU.add)
        rnd_ck["sHq"].append(p.plan["sHq"])
        return k

    r1free_ck, xqt_ck, xqe_ck, gu_last_ck, r1r_ck = [], [], [], [], []
    silu_ck, h_ck, hqe_ck, hw_ck = [], [], [], []
    hqbfree = []

    def emit_hquant_eighth(c, g):
        hs = hbuf[:, 512 * g:512 * (g + 1)]
        fc2 = 8 * (c % 2)
        ku = emit_round(hs, fcol[:, fc2 + 7:fc2 + 8], hs)
        if ku >= 2:
            W("vector", *hqbfree[ku - 2])
        W("tensor", "sHq", rnd_ck["sHq"][ku])
        quarter = g // 2
        bnk = 5 + (quarter % 2)
        gco = (g % 2) * 512
        if g % 2 == 0 and len(hqe_ck) >= 2:
            W("tensor", "sHqE", hqe_ck[-2])
        for j in range(4):
            O("tensor", "transpose", [pbb[bnk]], [hqb[:], eyeb[:]],
              sem=("sHqT" if j == 3 else None),
              out=pbb[bnk][:, gco + 128 * j:gco + 128 * (j + 1)],
              in_=hqb[:, (ku % 2) * 512 + 128 * j:
                      (ku % 2) * 512 + 128 * (j + 1)], identity=eyeb[:])
        hqbfree.append(("sHqT", p.plan["sHqT"]))
        if g % 2 == 1:
            qhalf = quarter % 2
            G = 4 * c + quarter
            if G // 2 >= 1:
                W("vector", *hw_ck[G // 2 - 1])
            W("vector", "sHqT", p.plan["sHqT"])
            O("vector", "tensor_copy", [hqt_st[:]], [pbb[bnk]],
              sem="sHqE",
              out=hqt_st[:, 1024 * qhalf:1024 * (qhalf + 1)],
              in_=pbb[bnk][:, :])
            hqe_ck.append(p.plan["sHqE"])
            if quarter % 2 == 1:
                half = quarter // 2
                W("scalar", "sHqE", hqe_ck[-1])
                hw_ck.append(p.pdma(
                    "scalar", "qhw",
                    hqTh[c, :, 2048 * half:2048 * (half + 1)],
                    hqt_st[:]))
        return rnd_ck["sHq"][ku]

    def emit_r1_load(cc):
        if cc >= 2:
            W("sync", "sHq", r1free_ck[cc - 2])
        W("sync", *r1w_ck[cc])
        r1r_ck.append(p.pdma("sync", "qr1r",
                             r1f[:, (cc % 2) * E:(cc % 2 + 1) * E],
                             r1h[cc * 128:(cc + 1) * 128, :]))

    def emit_head(cc):
        xp = cc % 2
        fc = 8 * xp
        r1c = r1f[:, xp * E:(xp + 1) * E]
        W("vector", *r1r_ck[cc])
        if cc >= 2:
            W("vector", "sSilu", silu_ck[8 * (cc - 2) + 7])
        O("vector", "tensor_reduce", [fcol[:]], [r1f[:]],
          out=fcol[:, fc:fc + 1], in_=r1c, axis=AXL.X, op=ALU.max,
          apply_absolute_value=True)
        O("vector", "tensor_scalar", [fcol[:]], [fcol[:]],
          out=fcol[:, fc + 1:fc + 2], in0=fcol[:, fc:fc + 1], scalar1=1e-5,
          scalar2=None, op0=ALU.max)
        O("vector", "reciprocal", [fcol[:]], [fcol[:]],
          out=fcol[:, fc + 2:fc + 3], in_=fcol[:, fc + 1:fc + 2])
        O("vector", "tensor_scalar", [fcol[:]], [fcol[:]], sem="sXq",
          out=fcol[:, fc + 3:fc + 4], in0=fcol[:, fc + 2:fc + 3],
          scalar1=127.0, scalar2=None, op0=ALU.mult)
        O("vector", "tensor_scalar", [fcol[:]], [fcol[:], mh127[:]],
          out=fcol[:, fc + 4:fc + 5], in0=fcol[:, fc + 1:fc + 2],
          scalar1=mh127[:, 0:1], scalar2=None, op0=ALU.mult)
        O("vector", "tensor_scalar", [fcol[:]], [fcol[:], mh127[:]],
          out=fcol[:, fc + 5:fc + 6], in0=fcol[:, fc + 1:fc + 2],
          scalar1=mh127[:, 1:2], scalar2=None, op0=ALU.mult)
        if cc > 0:
            W("tensor", "sXqE", xqe_ck[cc - 1])
        W("scalar", "sXq", p.plan["sXq"])
        for xh in range(2):
            ku = emit_round(r1c[:, 512 * xh:512 * (xh + 1)],
                            fcol[:, fc + 3:fc + 4], xt1[:])
            if ku >= 2:
                W("vector", *hqbfree[ku - 2])
            W("tensor", "sHq", rnd_ck["sHq"][ku])
            for j in range(4):
                O("tensor", "transpose", [pbb[0]], [hqb[:], eyeb[:]],
                  sem=("sXqT" if j == 3 else None),
                  out=pbb[0][:, 512 * xh + 128 * j:
                             512 * xh + 128 * (j + 1)],
                  in_=hqb[:, (ku % 2) * 512 + 128 * j:
                          (ku % 2) * 512 + 128 * (j + 1)],
                  identity=eyeb[:])
            xqt_ck.append(p.plan["sXqT"])
            hqbfree.append(("sXqT", p.plan["sXqT"]))
        r1free_ck.append(rnd_ck["sHq"][-1])
        W("vector", "sXqT", xqt_ck[-1])
        if cc >= 2:
            W("vector", "sGU", gu_last_ck[cc - 2])
        O("vector", "tensor_copy", [xqt[:]], [pbb[0]], sem="sXqE",
          out=xqt[:, xp * E:(xp + 1) * E], in_=pbb[0][:, :])
        xqe_ck.append(p.plan["sXqE"])

    emit_r1_load(0)
    emit_head(0)
    for c in range(NCH + 1):
        par = c % 2
        if c + 1 < NCH:
            emit_r1_load(c + 1)
        if c > 0:
            hp = (c - 1) % 2
            fc2 = 8 * hp
            if rnd_ck["sF1f"]:
                W("vector", "sF1f", rnd_ck["sF1f"][-1])
            O("vector", "tensor_reduce", [fcol[:]], [hmx8[:]],
              out=fcol[:, fc2 + 6:fc2 + 7], in_=hmx8[:, 8 * hp:8 * hp + 8],
              axis=AXL.X, op=ALU.max)
            O("vector", "tensor_scalar", [mxh_all[:]], [fcol[:]],
              out=mxh_all[:, c - 1:c], in0=fcol[:, fc2 + 6:fc2 + 7],
              scalar1=1e-5, scalar2=None, op0=ALU.max)
            O("vector", "reciprocal", [fcol[:]], [mxh_all[:]],
              out=fcol[:, fc2 + 7:fc2 + 8], in_=mxh_all[:, c - 1:c])
            O("vector", "tensor_scalar", [fcol[:]], [fcol[:]], sem="sXq",
              out=fcol[:, fc2 + 7:fc2 + 8], in0=fcol[:, fc2 + 7:fc2 + 8],
              scalar1=127.0, scalar2=None, op0=ALU.mult)
            W("scalar", "sXq", p.plan["sXq"])
        if c < NCH:
            W("tensor", "sXqE", xqe_ck[c])
        for n in range(NSTR):
            if c < NCH:
                gb, ub = (1, 2) if n % 2 == 0 else (3, 4)
                if c == 0:
                    W("tensor", "sTg", 8 * (n + 1))
                    W("tensor", "sTu", 8 * (n + 1))
                hidx = c * NSTR + n
                if hidx >= 2:
                    W("tensor", "sH", h_ck[hidx - 2])
                    W("tensor", "sSilu", silu_ck[hidx - 2])
                for e in range(ECH):
                    lt = xqt[:, par * E + 128 * e:par * E + 128 * (e + 1)]
                    O("tensor", "ldweights", [], [], weights=lt)
                    O("tensor", "matmul", [pb[gb].ap()], [xqt[:], wg_s[:]],
                      out=pb[gb][:, :], lhsT=lt,
                      rhs=wg3[:, e, 512 * n:512 * (n + 1)],
                      start=(e == 0), stop=(e == ECH - 1))
                    O("tensor", "matmul", [pb[ub].ap()], [xqt[:], wu_s[:]],
                      sem=("sGU" if e == ECH - 1 else None),
                      out=pb[ub][:, :], lhsT=lt,
                      rhs=wu3[:, e, 512 * n:512 * (n + 1)],
                      start=(e == 0), stop=(e == ECH - 1))
            if c > 0:
                emit_hquant_eighth(c - 1, n)
            if n == 0 and c + 1 < NCH:
                emit_head(c + 1)
            if c < NCH:
                spar = n % 2
                fc = 8 * par
                hidx = c * NSTR + n
                W("scalar", "sGU", p.plan["sGU"])
                if hidx >= 2:
                    W("scalar", "sH", h_ck[hidx - 2])
                O("scalar", "activation", [silub[:]],
                  [pb[gb].ap(), fcol[:], zcol[:]], sem="sSilu",
                  out=silub[:, spar * 512:(spar + 1) * 512], in_=pb[gb][:, :],
                  func=AF.Silu, scale=fcol[:, fc + 4:fc + 5], bias=zcol[:, 0:1])
                silu_ck.append(p.plan["sSilu"])
                W("vector", "sSilu", silu_ck[-1])
                O("vector", "scalar_tensor_tensor", [hbuf[:]],
                  [pb[ub].ap(), fcol[:], silub[:]],
                  out=hbuf[:, 512 * n:512 * (n + 1)], in0=pb[ub][:, :],
                  scalar=fcol[:, fc + 5:fc + 6],
                  in1=silub[:, spar * 512:(spar + 1) * 512],
                  op0=ALU.mult, op1=ALU.mult)
                O("vector", "tensor_reduce", [hmx8[:]], [hbuf[:]], sem="sH",
                  out=hmx8[:, 8 * par + n:8 * par + n + 1],
                  in_=hbuf[:, 512 * n:512 * (n + 1)],
                  axis=AXL.X, op=ALU.max, apply_absolute_value=True)
                h_ck.append(p.plan["sH"])
        if c < NCH:
            gu_last_ck.append(p.plan["sGU"])

    p.barrier()

    free("hmx8", "silub", "xt1", "xqt", "hqt_st", "hqb", "rres", "f1h",
         "q32h", "hbuf")
    wd3 = wg_s[:].rearrange("p (c f) -> p c f", c=ICH)
    hqL = wu_s
    rdw = wdTd.rearrange("(c p) f -> p c f", p=128)
    td_ck = []
    ta2_ck = []
    for i in range(ICH):
        for hh in range(2):
            idx = 2 * i + hh
            par_ = idx % 2
            if idx >= 2:
                W("sync", "sTd", td_ck[idx - 2])
            stg = wfst[:, par_ * 512:(par_ + 1) * 512]
            wf_ck.append(p.pdma("sync", "qwf", stg,
                                rdw[:, i, 512 * hh:512 * (hh + 1)]))
            W("scalar", *wf_ck[-1])
            if idx >= 2:
                W("scalar", "sTd", td_ck[idx - 2])
            O("scalar", "activation", [tern_a[:]], [wfst[:], nthb[:], onecol[:]],
              out=tern_a[:, par_ * 512:(par_ + 1) * 512], in_=stg,
              func=AF.Sign, bias=nthb[:, 2:3], scale=onecol[:, 0:1])
            O("scalar", "activation", [tern_b[:]], [wfst[:], thb[:], onecol[:]],
              sem="sTa",
              out=tern_b[:, par_ * 512:(par_ + 1) * 512], in_=stg,
              func=AF.Sign, bias=thb[:, 2:3], scale=onecol[:, 0:1])
            ta2_ck.append(p.plan["sTa"])
            W("vector", "sTa", ta2_ck[-1])
            O("vector", "tensor_tensor", [wg_s[:]], [tern_a[:], tern_b[:]],
              sem="sTd",
              out=wd3[:, i, 512 * hh:512 * (hh + 1)],
              in0=tern_a[:, par_ * 512:(par_ + 1) * 512],
              in1=tern_b[:, par_ * 512:(par_ + 1) * 512], op=ALU.add)
            td_ck.append(p.plan["sTd"])

    dn_ck, ep_ck, epr_ck, hr_ck, r1r2_ck, qo_ck = [], [], [], [], [], []
    W("sync", "qhw0", p.plan["qhw0"])
    W("sync", "qhw1", p.plan["qhw1"])
    W("sync", "qr1w0", p.plan["qr1w0"])
    W("sync", "qr1w1", p.plan["qr1w1"])
    for c in range(NCH):
        par = c % 2
        hqc = hqL[:, par * INNER:(par + 1) * INNER]
        if c >= 2:
            W("sync", "sDn", dn_ck[c - 2])
        hr_ck.append(p.pdma("sync", "qhr", hqc, hqTh[c, :, :]))
        if c >= 2:
            W("sync", "sEp", epr_ck[c - 2])
        r1r2_ck.append(p.pdma("sync", "qr1r",
                              r1f[:, par * E:(par + 1) * E],
                              r1h[c * 128:(c + 1) * 128, :]))
        W("tensor", *hr_ck[c])
        db0, db1 = (1, 2) if par == 0 else (3, 4)
        if c > 1:
            W("tensor", "sEp", epr_ck[c - 2])
        for i in range(ICH):
            if c == 0:
                W("tensor", "sTd", td_ck[2 * i + 1])
            lt = hqc[:, 128 * i:128 * (i + 1)]
            O("tensor", "ldweights", [], [], weights=lt)
            O("tensor", "matmul", [pb[db0].ap()], [hqL[:], wg_s[:]],
              out=pb[db0][:, :], lhsT=lt, rhs=wd3[:, i, 0:512],
              start=(i == 0), stop=(i == ICH - 1))
            O("tensor", "matmul", [pb[db1].ap()], [hqL[:], wg_s[:]],
              sem=("sDn" if i == ICH - 1 else None),
              out=pb[db1][:, :], lhsT=lt, rhs=wd3[:, i, 512:1024],
              start=(i == 0), stop=(i == ICH - 1))
        dn_ck.append(p.plan["sDn"])
        r1c = r1f[:, par * E:(par + 1) * E]
        W("vector", *r1r2_ck[c])
        W("vector", "sDn", dn_ck[c])
        if c > 0:
            W("vector", "sEp", ep_ck[c - 1])
        O("vector", "memset", [fcol[:]], [], ap=fcol[:, 1:3], constant=0.0)
        O("vector", "memset", [fcol[:]], [], ap=fcol[:, 4:5], constant=0.0)
        O("vector", "tensor_scalar", [fcol[:]], [mxh_all[:], mh127[:]],
          out=fcol[:, 0:1], in0=mxh_all[:, c:c + 1], scalar1=mh127[:, 2:3],
          scalar2=None, op0=ALU.mult)
        O("vector", "scalar_tensor_tensor", [r0b[:], fcol[:]],
          [pb[db0].ap(), r1f[:]],
          out=r0b[:, 0:512], in0=pb[db0][:, :], scalar=fcol[:, 0:1],
          in1=r1c[:, 0:512], op0=ALU.mult, op1=ALU.add,
          accum_out=fcol[:, 1:2])
        O("vector", "scalar_tensor_tensor", [r0b[:], fcol[:]],
          [pb[db1].ap(), r1f[:]], sem="sEp",
          out=r0b[:, 512:1024], in0=pb[db1][:, :], scalar=fcol[:, 0:1],
          in1=r1c[:, 512:1024], op0=ALU.mult, op1=ALU.add,
          accum_out=fcol[:, 2:3])
        epr_ck.append(p.plan["sEp"])
        O("vector", "tensor_scalar", [fcol[:]], [fcol[:]],
          out=fcol[:, 3:4], in0=fcol[:, 1:2], scalar1=fcol[:, 2:3],
          scalar2=None, op0=ALU.add)
        W("scalar", "sEp", p.plan["sEp"])
        O("scalar", "activation", [wfst[:], fcol[:]], [r0b[:]], sem="sEp",
          out=wfst[:], in_=r0b[:], func=AF.Square, accum_out=fcol[:, 4:5])
        W("vector", "sEp", p.plan["sEp"])
        O("vector", "tensor_scalar", [fcol[:]], [fcol[:]],
          out=fcol[:, 5:6], in0=fcol[:, 3:4], scalar1=fcol[:, 3:4],
          scalar2=-1.0 / (n_ln * n_ln), op0=ALU.mult, op1=ALU.mult)
        O("vector", "tensor_scalar", [fcol[:]], [fcol[:]],
          out=fcol[:, 5:6], in0=fcol[:, 5:6], scalar1=EPS_LN, scalar2=None,
          op0=ALU.add)
        O("vector", "tensor_scalar", [fcol[:]], [fcol[:]], sem="sEp",
          out=fcol[:, 6:7], in0=fcol[:, 4:5], scalar1=1.0 / n_ln,
          scalar2=fcol[:, 5:6], op0=ALU.mult, op1=ALU.add)
        W("scalar", "sEp", p.plan["sEp"])
        O("scalar", "activation", [fcol[:]], [fcol[:]], sem="sEp",
          out=fcol[:, 6:7], in_=fcol[:, 6:7], func=AF.Sqrt)
        W("vector", "sEp", p.plan["sEp"])
        O("vector", "reciprocal", [fcol[:]], [fcol[:]],
          out=fcol[:, 7:8], in_=fcol[:, 6:7])
        O("vector", "tensor_scalar", [fcol[:]], [fcol[:]], sem="sEp",
          out=fcol[:, 5:6], in0=fcol[:, 3:4], scalar1=-1.0 / n_ln,
          scalar2=fcol[:, 7:8], op0=ALU.mult, op1=ALU.mult)
        if c >= 2:
            W("vector", *qo_ck[c - 2])
        O("vector", "tensor_scalar", [r1b[:]], [r0b[:], fcol[:]], sem="sEp",
          out=r1b[:, par * E:(par + 1) * E], in0=r0b[:],
          scalar1=fcol[:, 7:8], scalar2=fcol[:, 5:6],
          op0=ALU.mult, op1=ALU.add)
        ep_ck.append(p.plan["sEp"])
        W("scalar", "sEp", ep_ck[c])
        qo_ck.append(p.pdma("scalar", "qo", out_d[c * 128:(c + 1) * 128, :],
                            r1b[:, par * E:(par + 1) * E]))

    if DBG:
        W("sync", "qo0", p.plan["qo0"])
        W("sync", "qo1", p.plan["qo1"])
        DMA("sync", dbg_r1, r1h, "qcc")
        DMA("sync", dbg_sf, cc_out.ap(), "qcc")
        DMA("sync", dbg_mh, mhb[:], "qcc")
        DMA("sync", dbg_hq, hqTh[0, :, :], "qcc")
        W("sync", "qcc", p.plan["qcc"])
    W("sync", "qo0", p.plan["qo0"])
    W("sync", "qo1", p.plan["qo1"])

    with nc.Block() as block:
        @block.sync
        def _(eng):
            for f in p.q["sync"]:
                f(eng)

        @block.scalar
        def _(eng):
            for f in p.q["scalar"]:
                f(eng)

        @block.vector
        def _(eng):
            for f in p.q["vector"]:
                f(eng)

        @block.tensor
        def _(eng):
            for f in p.q["tensor"]:
                f(eng)

        @block.gpsimd
        def _(eng):
            for f in p.q["gpsimd"]:
                f(eng)
    return nc, p


def kernel(L0_attn_query, L0_attn_Wq, L0_attn_bq, L0_attn_Wk, L0_attn_bk,
           L0_attn_Wv, L0_attn_bv, L0_attn_Wo, L0_attn_bo,
           L0_ffn_w_gate, L0_ffn_w_up, L0_ffn_w_down):
    x = np.ascontiguousarray(
        np.asarray(L0_attn_query, dtype=np.float32).reshape(B * L, E))
    asf = lambda a: np.ascontiguousarray(np.asarray(a, dtype=np.float32))
    Wq, Wk, Wv, Wo = map(asf, (L0_attn_Wq, L0_attn_Wk, L0_attn_Wv, L0_attn_Wo))
    bq, bk, bv, bo = map(asf, (L0_attn_bq, L0_attn_bk, L0_attn_bv, L0_attn_bo))
    wg, wu, wd = map(asf, (L0_ffn_w_gate, L0_ffn_w_up, L0_ffn_w_down))

    common = {
        "wqT": np.ascontiguousarray(Wq.T), "wkT": np.ascontiguousarray(Wk.T),
        "wvT": np.ascontiguousarray(Wv.T), "woT": np.ascontiguousarray(Wo.T),
        "bqpp": np.ascontiguousarray(bq.reshape(ECH, 128).T),
        "bkb": np.ascontiguousarray(np.broadcast_to(bk, (128, E))),
        "bob": np.ascontiguousarray(np.broadcast_to(bo, (128, E))),
        "bv2": np.ascontiguousarray(
            bv.reshape(NPAIR, 2, D).transpose(1, 0, 2).reshape(2, 512)),
        "wgT": np.ascontiguousarray(wg.T), "wuT": np.ascontiguousarray(wu.T),
        "wdT": np.ascontiguousarray(wd.T),
        "eyeb": np.eye(128, dtype=ml_dtypes.bfloat16),
        "eyef": np.eye(128, dtype=np.float32),
    }
    in_maps = []
    for c in range(NCORES):
        xc = x[c * TPC:(c + 1) * TPC, :]
        m = dict(common)
        m["x"] = np.ascontiguousarray(xc)
        m["xT"] = np.ascontiguousarray(xc.T)
        in_maps.append(m)

    nc, _ = build_full()
    res = run_bass_kernel_spmd(nc, in_maps, core_ids=list(range(NCORES)))
    outs = [np.asarray(res.results[c]["out"], dtype=np.float32)
            for c in range(NCORES)]
    return np.concatenate(outs, axis=0).reshape(B, L, E)


# version 15
# speedup vs baseline: 1.5743x; 1.5743x over previous
import numpy as np
import ml_dtypes
import concourse.bass as bass
import concourse.mybir as mybir
from concourse.bass_utils import run_bass_kernel_spmd

F32 = mybir.dt.float32
BF16 = mybir.dt.bfloat16
I32 = mybir.dt.int32
ALU = mybir.AluOpType
AF = mybir.ActivationFunctionType
AXL = mybir.AxisListType

B, L, E, H, D = 4, 8192, 1024, 16, 64
NCORES = 8
TPC = L * B // NCORES
NCH = TPC // 128
NSC = TPC // 512
ECH = E // 128
INNER = 4096
NSTR = INNER // 512
ICH = INNER // 128
NPAIR = 8
EPS_ATTN = 1e-6
EPS_LN = 1e-5
WCNT = float(INNER * E)

ENGINES = ("sync", "scalar", "vector", "tensor", "gpsimd")
DBG = False


class Prog:

    def __init__(self, nc):
        self.nc = nc
        self.q = {e: [] for e in ENGINES}
        self.sems = {}
        self.plan = {}
        self.wr = {e: set() for e in ENGINES}
        self.rd = {e: set() for e in ENGINES}
        self.dmak = {}

    def sem(self, name):
        h = self.nc.semaphore(name).__enter__()
        self.sems[name] = h
        self.plan[name] = 0

    @staticmethod
    def _names(aps):
        out = set()
        for a in aps:
            t = getattr(a, "tensor", None)
            if t is not None:
                out.add(t.name)
        return out

    def wait(self, eng, sem, value=None):
        if value is None:
            value = self.plan[sem]
        if value <= 0:
            return
        h = self.sems[sem]
        self.q[eng].append(lambda e, h=h, v=value: e.wait_ge(h, v))

    def op(self, eng, method, outs, ins, sem=None, **kw):
        onames = self._names(outs)
        inames = self._names(ins)
        if eng != "tensor" and (
                (inames & self.wr[eng]) or
                (onames & (self.wr[eng] | self.rd[eng]))):
            self.q[eng].append(lambda e: e.drain())
            self.wr[eng].clear()
            self.rd[eng].clear()
        self.wr[eng] |= onames
        self.rd[eng] |= inames
        semh = self.sems[sem] if sem else None
        if sem:
            self.plan[sem] += 1

        def f(e, method=method, kw=kw, semh=semh):
            inst = getattr(e, method)(**kw)
            if semh is not None:
                inst.then_inc(semh, 1)
        self.q[eng].append(f)

    def dma(self, eng, out, in_, sem):
        semh = self.sems[sem]
        self.plan[sem] += 16

        def f(e, out=out, in_=in_, semh=semh):
            e.dma_start(out=out, in_=in_).then_inc(semh, 16)
        self.q[eng].append(f)

    def pdma(self, eng, stream, out, in_):
        k = self.dmak.get(stream, 0)
        self.dmak[stream] = k + 1
        sem = f"{stream}{k % 2}"
        if k >= 2:
            self.wait(eng, sem, 16 * (k // 2))
        self.dma(eng, out, in_, sem)
        return (sem, 16 * (k // 2 + 1))

    def raw(self, eng, fn):
        self.q[eng].append(fn)

    def barrier(self, name="bar"):
        h = self.sems[name]
        self.plan[name] += 5
        v = self.plan[name]
        for eng in ENGINES:
            def f(e, h=h):
                e.drain()
                e.nop().then_inc(h, 1)
            self.q[eng].append(f)
            self.wr[eng].clear()
            self.rd[eng].clear()
        for eng in ENGINES:
            self.q[eng].append(lambda e, h=h, v=v: e.wait_ge(h, v))


def build_full():
    nc = bass.Bass("TRN2", target_bir_lowering=False, debug=False)
    p = Prog(nc)
    W, O, DMA = p.wait, p.op, p.dma

    ap = lambda t: t.ap()
    xT = ap(nc.dram_tensor("xT", [E, TPC], F32, kind="ExternalInput"))
    x_tm = ap(nc.dram_tensor("x", [TPC, E], F32, kind="ExternalInput"))
    wqT = ap(nc.dram_tensor("wqT", [E, E], F32, kind="ExternalInput"))
    wkT = ap(nc.dram_tensor("wkT", [E, E], F32, kind="ExternalInput"))
    wvT = ap(nc.dram_tensor("wvT", [E, E], F32, kind="ExternalInput"))
    woT = ap(nc.dram_tensor("woT", [E, E], F32, kind="ExternalInput"))
    bqpp = ap(nc.dram_tensor("bqpp", [128, ECH], F32, kind="ExternalInput"))
    bkb_d = ap(nc.dram_tensor("bkb", [128, E], F32, kind="ExternalInput"))
    bob_d = ap(nc.dram_tensor("bob", [128, E], F32, kind="ExternalInput"))
    bv2_d = ap(nc.dram_tensor("bv2", [2, 512], F32, kind="ExternalInput"))
    wgTd = ap(nc.dram_tensor("wgT", [E, INNER], F32, kind="ExternalInput"))
    wuTd = ap(nc.dram_tensor("wuT", [E, INNER], F32, kind="ExternalInput"))
    wdTd = ap(nc.dram_tensor("wdT", [INNER, E], F32, kind="ExternalInput"))
    eyeb_d = ap(nc.dram_tensor("eyeb", [128, 128], BF16, kind="ExternalInput"))
    eyef_d = ap(nc.dram_tensor("eyef", [128, 128], F32, kind="ExternalInput"))
    out_d = ap(nc.dram_tensor("out", [TPC, E], F32, kind="ExternalOutput"))

    if DBG:
        dbg_r1 = ap(nc.dram_tensor("dbg_r1", [TPC, E], F32,
                                   kind="ExternalOutput"))
        dbg_sf = ap(nc.dram_tensor("dbg_sf", [128, NPAIR * 132], F32,
                                   kind="ExternalOutput"))
        dbg_mh = ap(nc.dram_tensor("dbg_mh", [128, 3], F32,
                                   kind="ExternalOutput"))
        dbg_hq = ap(nc.dram_tensor("dbg_hq", [128, INNER], BF16,
                                   kind="ExternalOutput"))
    cc_in = nc.dram_tensor("cc_in", [128, NPAIR * 132], F32)
    cc_out = nc.dram_tensor("cc_out", [128, NPAIR * 132], F32)
    r1h = ap(nc.dram_tensor("r1h", [TPC, E], F32))
    wgt_h = ap(nc.dram_tensor("wgth", [E, INNER], BF16))
    wut_h = ap(nc.dram_tensor("wuth", [E, INNER], BF16))
    hqTh = ap(nc.dram_tensor("hqTh", [NCH, 128, INNER], BF16))

    live = {}

    def alloc(name, shape, dtype, side="left", psum=False):
        if psum:
            cm = nc.psum_tensor(name, shape, dtype)
        else:
            cm = nc.sbuf_tensor(name, shape, dtype, side=side)
        t = cm.__enter__()
        live[name] = cm
        return t

    def free(*names):
        for n in names:
            live.pop(n).__exit__(None, None, None)

    for s in ("qconst", "qwa", "qcc", "qccr", "ccd",
              "qxt0", "qxt1", "qws0", "qws1", "qx0", "qx1", "qr1w0", "qr1w1",
              "qr1r0", "qr1r1", "qwf0", "qwf1", "qhw0", "qhw1", "qhr0",
              "qtw0", "qtw1", "qwl0", "qwl1",
              "qhr1", "qo0", "qo1",
              "sKV", "sPhiD", "sPhiK", "sER", "sVc", "sS", "sWred",
              "sAf", "sQmm", "sER2", "sPhiQ", "sSb",
              "sNum", "sDiv", "sCtxT", "sCtxE", "sWo", "sXpb", "sR0", "sR1",
              "sTg", "sTu", "sTd",
              "sXq", "sXqT", "sXqE", "sGU", "sSilu", "sH", "sHq", "sHqT",
              "sHqE", "sDn", "sEp", "sF1f", "sRr", "sCd", "sTa", "bar"):
        p.sem(s)

    eyeb = alloc("eyeb_s", [128, 128], BF16)
    eyef = alloc("eyef_s", [128, 128], F32)
    bq_s = alloc("bq_s", [128, ECH], F32)
    onesb = alloc("onesb", [128, 1], BF16)
    onesr = alloc("onesr", [1, 128], F32)
    zcol = alloc("zcol", [128, 1], F32)
    onecol = alloc("onecol", [128, 1], F32)
    mhb = alloc("mhb", [128, 3], F32)
    thb = alloc("thb", [128, 3], F32)
    nthb = alloc("nthb", [128, 3], F32)
    mh127 = alloc("mh127", [128, 3], F32)
    mxh_all = alloc("mxh_all", [128, NCH], F32)
    tcol = alloc("tcol", [128, 16], F32)
    fcol = alloc("fcol", [128, 16], F32)
    r0b = alloc("r0b", [128, E], F32)
    r1b = alloc("r1b", [128, 2 * E], F32)

    class PV:
        def __init__(self, a):
            self._a = a

        def ap(self):
            return self._a

        def __getitem__(self, k):
            return self._a[k]

    pbig = alloc("pbig", [128, 4096], F32, psum=True)
    pb = [PV(pbig[:, 512 * k:512 * (k + 1)]) for k in range(8)]
    pbbig = pbig.ap().bitcast(BF16)
    pbb = [pbbig[:, 1024 * k:1024 * (k + 1)] for k in range(8)]

    DMA("sync", eyeb[:], eyeb_d, "qconst")
    DMA("sync", eyef[:], eyef_d, "qconst")
    DMA("sync", bq_s[:], bqpp, "qconst")
    O("vector", "memset", [onesb[:]], [], sem="sSb", ap=onesb[:], constant=1.0)
    O("vector", "memset", [onesr[:]], [], sem="sSb", ap=onesr[:], constant=1.0)
    O("vector", "memset", [zcol[:]], [], sem="sSb", ap=zcol[:], constant=0.0)
    O("vector", "memset", [onecol[:]], [], sem="sSb", ap=onecol[:], constant=1.0)

    bkb = alloc("bkb_s", [128, E], F32)
    bob = alloc("bob_s", [128, E], F32)
    bv2 = alloc("bv2_s", [2, 512], F32)
    ccs = alloc("ccs", [128, NPAIR * 132], F32)
    sfull = alloc("sfullb", [128, NPAIR * 130], BF16)
    wparts = alloc("wparts", [128, 48], F32)
    xqbuf = alloc("xqbuf", [128, ECH * TPC], BF16)

    phik = alloc("phik", [128, 2 * E], BF16, side="right")
    vbuf = alloc("vbuf", [128, 2 * E], BF16, side="right")
    kqbuf = alloc("kqbuf", [128, E], F32, side="right")
    ebuf = alloc("ebuf", [128, E], F32, side="right")
    rbuf = alloc("rbuf", [128, E], F32, side="right")
    wk_s = alloc("wk_s", [128, ECH * E], BF16, side="right")
    wv_s = alloc("wv_s", [128, ECH * E], BF16, side="right")
    wstage = alloc("wstage", [128, 2 * 2048], F32, side="right")

    DMA("sync", bkb[:], bkb_d, "qconst")
    DMA("sync", bob[:], bob_d, "qconst")
    DMA("sync", bv2[:], bv2_d, "qconst")
    O("vector", "memset", [sfull[:]], [], sem="sSb", ap=sfull[:], constant=0.0)
    O("vector", "memset", [ccs[:]], [], sem="sSb", ap=ccs[:], constant=0.0)
    for k in range(3):
        O("vector", "memset", [pb[k].ap()], [], sem="sSb",
          ap=pb[k][:, :], constant=0.0)
    for _e in ("tensor", "vector", "scalar", "gpsimd"):
        W(_e, "qconst", p.plan["qconst"])
        W(_e, "sSb", p.plan["sSb"])

    for wsrc, wdst in ((wkT, wk_s), (wvT, wv_s)):
        DMA("gpsimd", wdst[:].rearrange("p (c f) -> p c f", c=ECH),
            wsrc.rearrange("(c p) f -> p c f", p=128), "qwa")
    xr = xT.rearrange("(c p) t -> p c t", p=128)
    xq3 = xqbuf[:].rearrange("p (c t) -> p c t", c=ECH)
    xt_ck = []
    for sc in range(NSC):
        xt_ck.append(p.pdma("gpsimd", "qxt",
                            xq3[:, :, sc * 512:(sc + 1) * 512],
                            xr[:, :, sc * 512:(sc + 1) * 512]))

    wpieces = []
    for wd_ in (wgTd, wuTd, wdTd):
        nrowch = wd_.shape[0] // 128
        ncol = wd_.shape[1]
        r = wd_.rearrange("(c p) f -> p c f", p=128)
        if ncol >= 2048:
            for i in range(nrowch):
                for j in range(ncol // 2048):
                    wpieces.append(r[:, i:i + 1, 2048 * j:2048 * (j + 1)])
        else:
            per = 2048 // ncol
            for i in range(0, nrowch, per):
                wpieces.append(r[:, i:i + per, :])
    assert len(wpieces) == 48
    ws_ck = []

    wk3 = wk_s[:].rearrange("p (c f) -> p c f", c=ECH)
    wv3 = wv_s[:].rearrange("p (c f) -> p c f", c=ECH)
    sbank = [0, 0, 0, 1, 1, 1, 2, 2]
    scol = [0, 132, 264, 0, 132, 264, 0, 132]

    def emit_s_mms(tc):
        par = tc % 2
        W("tensor", "sPhiK")
        W("tensor", "sVc")
        for pr in range(NPAIR):
            bk_, co = sbank[pr], scol[pr]
            lt = phik[:, par * E + 128 * pr: par * E + 128 * (pr + 1)]
            O("tensor", "ldweights", [], [], weights=lt)
            O("tensor", "matmul", [pb[bk_].ap()], [],
              out=pb[bk_][:, co:co + 128], lhsT=lt,
              rhs=vbuf[:, par * E + 128 * pr: par * E + 128 * (pr + 1)],
              start=(tc == 0 and pr % 3 == 0), stop=False,
              skip_group_check=True)
            O("tensor", "matmul", [pb[bk_].ap()], [],
              sem=("sS" if pr == NPAIR - 1 else None),
              out=pb[bk_][:, co + 128:co + 129], lhsT=lt, rhs=onesb[:],
              start=False, stop=(tc == NCH - 1), skip_group_check=True)

    for tc in range(NCH):
        par = tc % 2
        sc = tc // 4
        W("tensor", *xt_ck[sc])
        if tc > 0:
            W("tensor", "sPhiD", tc)
            W("tensor", "sVc", tc)
        if tc == 0:
            W("tensor", "qwa", 32)
        for e in range(ECH):
            lt = xq3[:, e, tc * 128:(tc + 1) * 128]
            O("tensor", "ldweights", [], [], weights=lt)
            for half, bnk, w3 in ((0, 3, wk3), (1, 4, wk3),
                                  (0, 5, wv3), (1, 6, wv3)):
                O("tensor", "matmul", [pb[bnk].ap()], [],
                  sem=("sKV" if (e == ECH - 1 and bnk == 6) else None),
                  out=pb[bnk][:, :], lhsT=lt,
                  rhs=w3[:, e, half * 512:(half + 1) * 512],
                  start=(e == 0), stop=(e == ECH - 1))
        if tc > 0:
            emit_s_mms(tc - 1)

        if tc >= 1:
            W("vector", "sER")
        W("vector", "sKV", tc + 1)
        O("vector", "tensor_tensor", [kqbuf[:]], [pb[3].ap(), bkb[:]],
          out=kqbuf[:, 0:512], in0=pb[3][:, :], in1=bkb[:, 0:512], op=ALU.add)
        O("vector", "tensor_tensor", [kqbuf[:]], [pb[4].ap(), bkb[:]],
          sem="sPhiD",
          out=kqbuf[:, 512:1024], in0=pb[4][:, :], in1=bkb[:, 512:1024],
          op=ALU.add)
        if tc >= 1:
            W("scalar", "sPhiK")
        W("scalar", "sPhiD", p.plan["sPhiD"])
        O("scalar", "activation", [ebuf[:]], [kqbuf[:]],
          out=ebuf[:], in_=kqbuf[:], func=AF.Exp)
        O("scalar", "activation", [rbuf[:]], [kqbuf[:]], sem="sER",
          out=rbuf[:], in_=kqbuf[:], func=AF.Relu)
        if tc >= 2:
            W("vector", "sS", tc - 1)
        W("vector", "sER", p.plan["sER"])
        O("vector", "scalar_tensor_tensor", [phik[:]], [ebuf[:], rbuf[:]],
          sem="sPhiK",
          out=phik[:, par * E:(par + 1) * E], in0=ebuf[:], scalar=1.0,
          in1=rbuf[:], op0=ALU.min, op1=ALU.add)
        W("scalar", "sKV", p.plan["sKV"])
        if tc >= 2:
            W("scalar", "sS", tc - 1)
        O("scalar", "copy", [vbuf[:]], [pb[5].ap()],
          out=vbuf[:, par * E:par * E + 512], in_=pb[5][:, :])
        O("scalar", "copy", [vbuf[:]], [pb[6].ap()], sem="sVc",
          out=vbuf[:, par * E + 512:(par + 1) * E], in_=pb[6][:, :])

        idxs = [i for i in range(48) if i * 32 // 48 == tc]
        for idx in idxs:
            bufp = idx % 2
            if idx >= 2:
                W("sync", "sWred", idx - 1)
            ws_ck.append(p.pdma("sync", "qws",
                                wstage[:, bufp * 2048:(bufp + 1) * 2048],
                                wpieces[idx]))
            W("vector", *ws_ck[idx])
            O("vector", "tensor_reduce", [wparts[:]], [wstage[:]], sem="sWred",
              out=wparts[:, idx:idx + 1],
              in_=wstage[:, bufp * 2048:(bufp + 1) * 2048],
              axis=AXL.X, op=ALU.add, apply_absolute_value=True)

    emit_s_mms(NCH - 1)

    W("vector", "sWred", p.plan["sWred"])
    for w in range(3):
        O("vector", "tensor_reduce", [tcol[:]], [wparts[:]],
          sem=("sAf" if w == 2 else None),
          out=tcol[:, w:w + 1], in_=wparts[:, 16 * w:16 * (w + 1)],
          axis=AXL.X, op=ALU.add)
    W("tensor", "sAf", p.plan["sAf"])
    W("tensor", "sS", p.plan["sS"])
    O("tensor", "transpose", [pb[7].ap()], [tcol[:], eyef[:]], sem="sAf",
      out=pb[7][0:3, 0:128], in_=tcol[:, 0:3], identity=eyef[:])
    W("vector", "sAf", p.plan["sAf"])
    O("vector", "tensor_reduce", [tcol[:]], [pb[7].ap()],
      out=tcol[0:3, 3:4], in_=pb[7][0:3, 0:128], axis=AXL.X, op=ALU.add)
    O("vector", "tensor_scalar", [tcol[:]], [tcol[:]], sem="sAf",
      out=tcol[0:3, 4:5], in0=tcol[0:3, 3:4], scalar1=1.0 / WCNT,
      scalar2=1e-5, op0=ALU.mult, op1=ALU.max)
    W("tensor", "sAf", p.plan["sAf"])
    O("tensor", "transpose", [pb[6].ap()], [tcol[:], eyef[:]], sem="sAf",
      out=pb[6][0:1, 0:3], in_=tcol[0:3, 4:5], identity=eyef[0:3, 0:3])
    W("vector", "sAf", p.plan["sAf"])
    O("vector", "tensor_copy", [tcol[:]], [pb[6].ap()], sem="sAf",
      out=tcol[0:1, 5:8], in_=pb[6][0:1, 0:3])
    W("tensor", "sAf", p.plan["sAf"])
    O("tensor", "matmul", [pb[5].ap()], [onesr[:], tcol[:]], sem="sAf",
      out=pb[5][:, 0:3], lhsT=onesr[:], rhs=tcol[0:1, 5:8],
      start=True, stop=True)
    W("vector", "sAf", p.plan["sAf"])
    O("vector", "tensor_copy", [mhb[:]], [pb[5].ap()],
      out=mhb[:], in_=pb[5][:, 0:3])
    O("vector", "tensor_scalar", [thb[:]], [mhb[:]],
      out=thb[:], in0=mhb[:], scalar1=0.5, scalar2=None, op0=ALU.mult)
    O("vector", "tensor_scalar", [nthb[:]], [mhb[:]],
      out=nthb[:], in0=mhb[:], scalar1=-0.5, scalar2=None, op0=ALU.mult)
    O("vector", "tensor_scalar", [mh127[:]], [mhb[:]], sem="sAf",
      out=mh127[:], in0=mhb[:], scalar1=1.0 / 254.0, scalar2=None,
      op0=ALU.mult)

    W("vector", "sS", p.plan["sS"])
    for k in range(3):
        ncols = 396 if k < 2 else 264
        O("vector", "tensor_copy", [ccs[:]], [pb[k].ap()],
          sem=("sSb" if k == 2 else None),
          out=ccs[:, 396 * k:396 * k + ncols], in_=pb[k][:, 0:ncols])
    W("sync", "sSb", p.plan["sSb"])
    DMA("sync", cc_in.ap(), ccs[:], "qcc")
    free("wstage", "wv_s", "wk_s")
    phiqt = alloc("phiqt", [128, ECH * TPC], BF16)
    wq_s = alloc("wq_s", [128, ECH * E], BF16, side="right")
    wq3 = wq_s[:].rearrange("p (c f) -> p c f", c=ECH)
    W("gpsimd", "qcc", p.plan["qcc"])

    def _cc(e):
        e.collective_compute(
            "AllReduce", ALU.add,
            replica_groups=[[0, 1], [2, 3], [4, 5], [6, 7]],
            ins=[cc_in.ap().opt()], outs=[cc_out.ap().opt()],
        ).then_inc(p.sems["ccd"], 1)
    p.raw("gpsimd", _cc)
    p.plan["ccd"] += 1
    W("gpsimd", "sS", p.plan["sS"])
    W("gpsimd", "sVc", p.plan["sVc"])
    DMA("gpsimd", wq_s[:].rearrange("p (c f) -> p c f", c=ECH),
        wqT.rearrange("(c p) f -> p c f", p=128), "qwa")
    W("gpsimd", "ccd", p.plan["ccd"])
    DMA("gpsimd", ccs[:], cc_out.ap(), "qccr")

    pq3 = phiqt[:].rearrange("p (c t) -> p c t", c=ECH)

    qmm_ck, phiq_ck, er2_ck = [], [], []
    W("tensor", "qwa", p.plan["qwa"])
    W("tensor", "sSb", p.plan["sSb"])
    W("tensor", "sAf", p.plan["sAf"])
    for sc in range(NSC):
        for f in range(ECH):
            i = sc * ECH + f
            if sc > 0:
                W("tensor", "sPhiQ", phiq_ck[(sc - 1) * ECH + f])
            for e in range(ECH):
                lt = wq3[:, e, 128 * f:128 * (f + 1)]
                O("tensor", "ldweights", [], [], weights=lt)
                O("tensor", "matmul", [pb[f].ap()], [],
                  sem=("sQmm" if e == ECH - 1 else None),
                  out=pb[f][:, :], lhsT=lt,
                  rhs=xq3[:, e, sc * 512:(sc + 1) * 512],
                  start=(e == 0), stop=(e == ECH - 1))
            qmm_ck.append(p.plan["sQmm"])
            par = f % 2
            if i >= 2:
                W("scalar", "sPhiQ", phiq_ck[i - 2])
            W("scalar", "sQmm", qmm_ck[i])
            O("scalar", "activation", [ebuf[:]], [pb[f].ap(), bq_s[:], onecol[:]],
              out=ebuf[:, par * 512:(par + 1) * 512], in_=pb[f][:, :],
              func=AF.Exp, bias=bq_s[:, f:f + 1], scale=onecol[:, 0:1])
            O("scalar", "activation", [rbuf[:]], [pb[f].ap(), bq_s[:], onecol[:]],
              sem="sER2",
              out=rbuf[:, par * 512:(par + 1) * 512], in_=pb[f][:, :],
              func=AF.Relu, bias=bq_s[:, f:f + 1], scale=onecol[:, 0:1])
            er2_ck.append(p.plan["sER2"])
            W("vector", "sER2", er2_ck[i])
            O("vector", "scalar_tensor_tensor", [pq3], [ebuf[:], rbuf[:]],
              sem="sPhiQ",
              out=pq3[:, f, sc * 512:(sc + 1) * 512],
              in0=ebuf[:, par * 512:(par + 1) * 512], scalar=1.0,
              in1=rbuf[:, par * 512:(par + 1) * 512],
              op0=ALU.min, op1=ALU.add)
            phiq_ck.append(p.plan["sPhiQ"])

    zb2a = alloc("zb2a", [128, 16], F32, side="right")
    zall = alloc("zall", [2, 1024], F32, side="right")
    W("vector", "qccr", p.plan["qccr"])
    W("tensor", "sPhiQ", p.plan["sPhiQ"])
    O("vector", "memset", [zb2a[:]], [], ap=zb2a[:], constant=0.0)
    for pr in range(NPAIR):
        base = 132 * pr
        O("vector", "tensor_copy", [zb2a[:]], [ccs[:]],
          out=zb2a[0:64, 2 * pr:2 * pr + 1],
          in_=ccs[0:64, base + 128:base + 129])
        O("vector", "tensor_copy", [zb2a[:]], [ccs[:]],
          sem=("sSb" if pr == NPAIR - 1 else None),
          out=zb2a[64:128, 2 * pr + 1:2 * pr + 2],
          in_=ccs[64:128, base + 128:base + 129])
    W("tensor", "sSb", p.plan["sSb"])
    for pr in range(NPAIR):
        bnk = 7 if pr < 4 else 3
        O("tensor", "transpose", [pb[bnk].ap()], [zb2a[:], eyef[:]],
          sem=("sSb" if pr % 4 == 3 else None),
          out=pb[bnk][0:2, 128 * (pr % 4):128 * (pr % 4) + 128],
          in_=zb2a[:, 2 * pr:2 * pr + 2], identity=eyef[:])
    W("vector", "sSb", p.plan["sSb"])
    O("vector", "tensor_copy", [zall[:]], [pb[7].ap()],
      out=zall[:, 0:512], in_=pb[7][0:2, 0:512])
    O("vector", "tensor_copy", [zall[:]], [pb[3].ap()], sem="sSb",
      out=zall[:, 512:1024], in_=pb[3][0:2, 0:512])
    W("tensor", "sSb", p.plan["sSb"])
    for pr in range(NPAIR):
        O("tensor", "matmul", [pb[6].ap()], [zall[:], bv2[:]],
          sem=("sSb" if pr == NPAIR - 1 else None),
          out=pb[6][:, 64 * pr:64 * (pr + 1)],
          lhsT=zall[:, 128 * pr:128 * pr + 128],
          rhs=bv2[:, 64 * pr:64 * (pr + 1)],
          start=(pr == 0), stop=(pr == NPAIR - 1), skip_group_check=True)
    W("vector", "sSb", p.plan["sSb"])
    for pr in range(NPAIR):
        base = 132 * pr
        sb_ = 130 * pr
        O("vector", "tensor_tensor", [sfull[:]], [ccs[:], pb[6].ap()],
          out=sfull[0:64, sb_:sb_ + 64], in0=ccs[0:64, base:base + 64],
          in1=pb[6][0:64, 64 * pr:64 * pr + 64], op=ALU.add)
        O("vector", "tensor_tensor", [sfull[:]], [ccs[:], pb[6].ap()],
          out=sfull[64:128, sb_ + 64:sb_ + 128],
          in0=ccs[64:128, base + 64:base + 128],
          in1=pb[6][64:128, 64 * pr:64 * pr + 64], op=ALU.add)
        O("vector", "tensor_copy", [sfull[:]], [ccs[:]],
          out=sfull[0:64, sb_ + 128:sb_ + 129],
          in_=ccs[0:64, base + 128:base + 129])
        O("vector", "tensor_copy", [sfull[:]], [ccs[:]],
          sem=("sSb" if pr == NPAIR - 1 else None),
          out=sfull[64:128, sb_ + 129:sb_ + 130],
          in_=ccs[64:128, base + 128:base + 129])
    W("tensor", "sSb", p.plan["sSb"])
    free("zall", "zb2a")

    p.barrier()

    free("wq_s", "rbuf", "ebuf", "kqbuf", "vbuf", "phik")
    wo_s = alloc("wo_s", [128, ECH * E], BF16, side="right")
    xch = alloc("xch", [128, 2 * E], F32, side="right")
    xpb = alloc("xpb", [128, 2 * E], F32, side="right")
    ctx = alloc("ctx", [128, 2 * E], BF16, side="right")
    ctxt = alloc("ctxt", [128, 2 * E], BF16, side="right")
    dgb = alloc("dgb", [128, 32], F32, side="right")
    rden = alloc("rden", [128, 32], F32, side="right")
    r0b2 = alloc("r0b2", [128, E], F32, side="right")
    r0p = [r0b[:], r0b2[:]]
    DMA("gpsimd", wo_s[:].rearrange("p (c f) -> p c f", c=ECH),
        woT.rearrange("(c p) f -> p c f", p=128), "qwa")
    wo3 = wo_s[:].rearrange("p (c f) -> p c f", c=ECH)
    sf3 = sfull[:].rearrange("p (r c) -> p r c", r=NPAIR)
    n_ln = float(E)

    num_ck, div_ck, ctxt_ck, ctxe_ck, wo_ck = [], [], [], [], []
    r0_ck, r1_ck, xpb_ck, sq_ck, qx_ck, r1w_ck = [], [], [], [], [], []

    W("tensor", "qwa", p.plan["qwa"])
    for c in range(NCH):
        par = c % 2
        tc8 = 8 * par
        xpc = xpb[:, par * E:(par + 1) * E]
        r0c = r0p[par]
        ctxc = ctx[:, par * E:(par + 1) * E]
        ctxtc = ctxt[:, par * E:(par + 1) * E]
        if c >= 2:
            W("sync", "sXpb", xpb_ck[c - 2])
        qx_ck.append(p.pdma("sync", "qx", xch[:, par * E:(par + 1) * E],
                            x_tm[c * 128:(c + 1) * 128, :]))
        if c > 0:
            W("tensor", "sDiv", div_ck[c - 1])
        for pr in range(NPAIR):
            bk_, co = pr // 2, 256 * (pr % 2)
            lt = pq3[:, pr, c * 128:(c + 1) * 128]
            O("tensor", "ldweights", [], [], weights=lt)
            O("tensor", "matmul", [pb[bk_].ap()], [sfull[:]],
              sem=("sNum" if pr == NPAIR - 1 else None),
              out=pb[bk_][:, co:co + 130], lhsT=lt, rhs=sf3[:, pr, :],
              start=(pr % 2 == 0), stop=True, skip_group_check=True)
        num_ck.append(p.plan["sNum"])
        W("vector", "sNum", num_ck[c])
        numv = pbig.ap()[:, 0:2048].rearrange("p (r f) -> p r f", r=8)
        O("vector", "tensor_scalar", [dgb[:]], [pbig.ap()],
          out=dgb[:, 16 * par:16 * par + 16].rearrange(
              "p (r a) -> p r a", r=8),
          in0=numv[:, :, 128:130], scalar1=EPS_ATTN, scalar2=None,
          op0=ALU.add)
        O("vector", "reciprocal", [rden[:]], [dgb[:]],
          out=rden[:, 16 * par:16 * par + 16],
          in_=dgb[:, 16 * par:16 * par + 16])
        if c >= 2:
            W("vector", "sCtxT", ctxt_ck[c - 2])
        O("vector", "tensor_tensor", [ctx[:]], [pbig.ap(), rden[:]],
          sem="sDiv",
          out=ctxc.rearrange("p (r a b) -> p r a b", r=8, a=2),
          in0=numv[:, :, 0:128].rearrange("p r (a b) -> p r a b", a=2),
          in1=rden[:, 16 * par:16 * par + 16].rearrange(
              "p (r a) -> p r a", r=8).broadcast_to([128, 8, 2, 64]),
          op=ALU.mult)
        div_ck.append(p.plan["sDiv"])
        W("tensor", "sDiv", div_ck[c])
        if c > 0:
            W("tensor", "sCtxE", ctxe_ck[c - 1])
        for j in range(ECH):
            O("tensor", "transpose", [pbb[4]], [ctx[:], eyeb[:]],
              sem=("sCtxT" if j == ECH - 1 else None),
              out=pbb[4][:, 128 * j:128 * (j + 1)],
              in_=ctxc[:, 128 * j:128 * (j + 1)], identity=eyeb[:])
        ctxt_ck.append(p.plan["sCtxT"])
        W("scalar", "sCtxT", ctxt_ck[c])
        if c >= 2:
            W("scalar", "sWo", wo_ck[c - 2])
        O("scalar", "copy", [ctxt[:]], [pbb[4]], sem="sCtxE",
          out=ctxtc, in_=pbb[4][:, :])
        ctxe_ck.append(p.plan["sCtxE"])
        W("tensor", "sCtxE", ctxe_ck[c])
        if c > 0:
            W("tensor", "sR0", r0_ck[c - 1])
        for e in range(ECH):
            lt = ctxtc[:, 128 * e:128 * (e + 1)]
            O("tensor", "ldweights", [], [], weights=lt)
            for half, bnk in ((0, 5), (1, 6)):
                O("tensor", "matmul", [pb[bnk].ap()], [ctxt[:]],
                  sem=("sWo" if (e == ECH - 1 and half == 1) else None),
                  out=pb[bnk][:, :], lhsT=lt,
                  rhs=wo3[:, e, half * 512:(half + 1) * 512],
                  start=(e == 0), stop=(e == ECH - 1))
        wo_ck.append(p.plan["sWo"])
        W("gpsimd", *qx_ck[c])
        if c >= 2:
            W("gpsimd", "sR1", sq_ck[c - 2])
        O("gpsimd", "tensor_tensor", [xpb[:]], [xch[:], bob[:]], sem="sXpb",
          out=xpc, in0=xch[:, par * E:(par + 1) * E], in1=bob[:],
          op=ALU.add)
        xpb_ck.append(p.plan["sXpb"])
        W("vector", "sWo", wo_ck[c])
        W("vector", "sXpb", xpb_ck[c])
        if c >= 2:
            W("vector", "sR1", r1_ck[c - 2])
        O("vector", "memset", [tcol[:]], [], ap=tcol[:, tc8:tc8 + 2],
          constant=0.0)
        O("vector", "memset", [tcol[:]], [], ap=tcol[:, tc8 + 3:tc8 + 4],
          constant=0.0)
        O("vector", "scalar_tensor_tensor", [r0c, tcol[:]],
          [pb[5].ap(), xpb[:]],
          out=r0c[:, 0:512], in0=pb[5][:, :], scalar=1.0, in1=xpc[:, 0:512],
          op0=ALU.mult, op1=ALU.add, accum_out=tcol[:, tc8:tc8 + 1])
        O("vector", "scalar_tensor_tensor", [r0c, tcol[:]],
          [pb[6].ap(), xpb[:]], sem="sR0",
          out=r0c[:, 512:1024], in0=pb[6][:, :], scalar=1.0,
          in1=xpc[:, 512:1024], op0=ALU.mult, op1=ALU.add,
          accum_out=tcol[:, tc8 + 1:tc8 + 2])
        r0_ck.append(p.plan["sR0"])
        O("vector", "tensor_scalar", [tcol[:]], [tcol[:]],
          out=tcol[:, tc8 + 2:tc8 + 3], in0=tcol[:, tc8:tc8 + 1],
          scalar1=tcol[:, tc8 + 1:tc8 + 2], scalar2=None, op0=ALU.add)
        W("scalar", "sR0", r0_ck[c])
        O("scalar", "activation", [xpb[:], tcol[:]], [r0c], sem="sR1",
          out=xpc, in_=r0c, func=AF.Square, accum_out=tcol[:, tc8 + 3:tc8 + 4])
        sq_ck.append(p.plan["sR1"])
        W("vector", "sR1", p.plan["sR1"])
        O("vector", "tensor_scalar", [tcol[:]], [tcol[:]],
          out=tcol[:, tc8 + 4:tc8 + 5], in0=tcol[:, tc8 + 2:tc8 + 3],
          scalar1=tcol[:, tc8 + 2:tc8 + 3], scalar2=-1.0 / (n_ln * n_ln),
          op0=ALU.mult, op1=ALU.mult)
        O("vector", "tensor_scalar", [tcol[:]], [tcol[:]],
          out=tcol[:, tc8 + 4:tc8 + 5], in0=tcol[:, tc8 + 4:tc8 + 5],
          scalar1=EPS_LN, scalar2=None, op0=ALU.add)
        O("vector", "tensor_scalar", [tcol[:]], [tcol[:]], sem="sR1",
          out=tcol[:, tc8 + 5:tc8 + 6], in0=tcol[:, tc8 + 3:tc8 + 4],
          scalar1=1.0 / n_ln, scalar2=tcol[:, tc8 + 4:tc8 + 5],
          op0=ALU.mult, op1=ALU.add)
        W("scalar", "sR1", p.plan["sR1"])
        O("scalar", "activation", [tcol[:]], [tcol[:]], sem="sR1",
          out=tcol[:, tc8 + 6:tc8 + 7], in_=tcol[:, tc8 + 5:tc8 + 6],
          func=AF.Sqrt)
        W("vector", "sR1", p.plan["sR1"])
        O("vector", "reciprocal", [tcol[:]], [tcol[:]],
          out=tcol[:, tc8 + 7:tc8 + 8], in_=tcol[:, tc8 + 6:tc8 + 7])
        O("vector", "tensor_scalar", [tcol[:]], [tcol[:]], sem="sR1",
          out=tcol[:, tc8:tc8 + 1], in0=tcol[:, tc8 + 2:tc8 + 3],
          scalar1=-1.0 / n_ln, scalar2=tcol[:, tc8 + 7:tc8 + 8],
          op0=ALU.mult, op1=ALU.mult)
        if c >= 2:
            W("vector", *r1w_ck[c - 2])
        O("vector", "tensor_scalar", [r1b[:]], [r0c, tcol[:]], sem="sR1",
          out=r1b[:, par * E:(par + 1) * E], in0=r0c,
          scalar1=tcol[:, tc8 + 7:tc8 + 8], scalar2=tcol[:, tc8:tc8 + 1],
          op0=ALU.mult, op1=ALU.add)
        r1_ck.append(p.plan["sR1"])
        W("scalar", "sR1", r1_ck[c])
        r1w_ck.append(p.pdma("scalar", "qr1w", r1h[c * 128:(c + 1) * 128, :],
                             r1b[:, par * E:(par + 1) * E]))

    p.barrier()

    free("r0b2", "rden", "dgb", "ctxt", "ctx", "xpb", "xch", "wo_s")
    free("phiqt", "xqbuf", "wparts", "sfullb", "ccs", "bv2_s",
         "bob_s", "bkb_s")
    wg_s = alloc("wg_s", [128, ECH * INNER], BF16)
    wu_s = alloc("wu_s", [128, ECH * INNER], BF16)
    r1f = alloc("r1f", [128, 2 * E], F32)
    wfst = alloc("wfst2", [128, 1024], F32)
    tern_a = alloc("tern_a2", [128, 2 * 512], BF16)
    tern_b = alloc("tern_b2", [128, 2 * 512], BF16)
    hbuf = alloc("hbuf", [128, INNER], F32, side="right")
    q32h = alloc("q32h", [128, 2 * 512], I32, side="right")
    f1h = alloc("f1h", [128, 2 * 512], F32, side="right")
    rres = alloc("rres", [128, 2 * 512], F32, side="right")
    hqb = alloc("hqb", [128, 2 * 512], BF16, side="right")
    hqt_st = alloc("hqt_st", [128, 2048], BF16, side="right")
    xqt = alloc("xqt", [128, 2 * E], BF16, side="right")
    xt1 = alloc("xt1", [128, 512], F32, side="right")
    silub = alloc("silub", [128, 1024], F32, side="right")
    hmx8 = alloc("hmx8", [128, 16], F32, side="right")

    wg3 = wg_s[:].rearrange("p (c f) -> p c f", c=ECH)
    wu3 = wu_s[:].rearrange("p (c f) -> p c f", c=ECH)
    fpieces = []
    for wi, wd_ in ((0, wgTd), (1, wuTd)):
        r = wd_.rearrange("(c p) f -> p c f", p=128)
        for n in range(NSTR):
            for e in range(ECH):
                fpieces.append((wi, n, e, r[:, e, 512 * n:512 * (n + 1)]))
    tern_ck, ta_ck, wf_ck = [], [], []
    for idx, (wi, n, e, src_) in enumerate(fpieces):
        wsem = "sTg" if wi == 0 else "sTu"
        par_ = idx % 2
        if idx >= 2:
            W("sync", "sTa", ta_ck[idx - 2])
        stg = wfst[:, par_ * 512:(par_ + 1) * 512]
        wf_ck.append(p.pdma("sync", "qwf", stg, src_))
        W("scalar", *wf_ck[-1])
        if idx >= 2:
            W("scalar", "sTg" if fpieces[idx - 2][0] == 0 else "sTu",
              tern_ck[idx - 2])
        O("scalar", "activation", [tern_a[:]], [wfst[:], nthb[:], onecol[:]],
          out=tern_a[:, par_ * 512:(par_ + 1) * 512], in_=stg, func=AF.Sign,
          bias=nthb[:, wi:wi + 1], scale=onecol[:, 0:1])
        O("scalar", "activation", [tern_b[:]], [wfst[:], thb[:], onecol[:]],
          sem="sTa",
          out=tern_b[:, par_ * 512:(par_ + 1) * 512], in_=stg, func=AF.Sign,
          bias=thb[:, wi:wi + 1], scale=onecol[:, 0:1])
        ta_ck.append(p.plan["sTa"])
        W("vector", "sTa", ta_ck[-1])
        w_sb3 = wg3 if wi == 0 else wu3
        O("vector", "tensor_tensor", [w_sb3], [tern_a[:], tern_b[:]],
          sem=wsem,
          out=w_sb3[:, e, 512 * n:512 * (n + 1)],
          in0=tern_a[:, par_ * 512:(par_ + 1) * 512],
          in1=tern_b[:, par_ * 512:(par_ + 1) * 512], op=ALU.add)
        tern_ck.append(p.plan[wsem])

    rnd_ck = {"sF1f": [], "sHq": []}
    rnd_n = [0]

    def emit_round(src, s_ap, t1_ap):
        k = rnd_n[0]
        rnd_n[0] += 1
        pi = k % 2
        if k >= 2:
            W("scalar", "sHq", rnd_ck["sHq"][k - 2])
        O("scalar", "activation", [q32h[:]], [src, s_ap],
          out=q32h[:, pi * 512:(pi + 1) * 512], in_=src, func=AF.Copy,
          scale=s_ap, bias=0.0)
        O("scalar", "activation", [f1h[:]], [q32h[:]], sem="sF1f",
          out=f1h[:, pi * 512:(pi + 1) * 512],
          in_=q32h[:, pi * 512:(pi + 1) * 512], func=AF.Copy,
          scale=1.0, bias=-1.0)
        rnd_ck["sF1f"].append(p.plan["sF1f"])
        W("vector", "sF1f", rnd_ck["sF1f"][k])
        O("vector", "scalar_tensor_tensor", [rres[:]], [src, s_ap, f1h[:]],
          out=rres[:, pi * 512:(pi + 1) * 512], in0=src, scalar=s_ap,
          in1=f1h[:, pi * 512:(pi + 1) * 512],
          op0=ALU.mult, op1=ALU.subtract)
        O("vector", "scalar_tensor_tensor", [t1_ap], [rres[:], f1h[:]],
          out=t1_ap, in0=rres[:, pi * 512:(pi + 1) * 512], scalar=1.5,
          in1=f1h[:, pi * 512:(pi + 1) * 512], op0=ALU.is_gt, op1=ALU.add)
        O("vector", "scalar_tensor_tensor",
          [hqb[:]], [rres[:], t1_ap], sem="sHq",
          out=hqb[:, pi * 512:(pi + 1) * 512],
          in0=rres[:, pi * 512:(pi + 1) * 512], scalar=0.5,
          in1=t1_ap, op0=ALU.is_ge, op1=ALU.add)
        rnd_ck["sHq"].append(p.plan["sHq"])
        return k

    r1free_ck, xqt_ck, xqe_ck, gu_last_ck, r1r_ck = [], [], [], [], []
    silu_ck, h_ck, hqe_ck, hw_ck = [], [], [], []
    hqbfree = []

    def emit_hquant_eighth(c, g):
        hs = hbuf[:, 512 * g:512 * (g + 1)]
        fc2 = 8 * (c % 2)
        ku = emit_round(hs, fcol[:, fc2 + 7:fc2 + 8], hs)
        if ku >= 2:
            W("vector", *hqbfree[ku - 2])
        W("tensor", "sHq", rnd_ck["sHq"][ku])
        quarter = g // 2
        bnk = 5 + (quarter % 2)
        gco = (g % 2) * 512
        if g % 2 == 0 and len(hqe_ck) >= 2:
            W("tensor", "sHqE", hqe_ck[-2])
        for j in range(4):
            O("tensor", "transpose", [pbb[bnk]], [hqb[:], eyeb[:]],
              sem=("sHqT" if j == 3 else None),
              out=pbb[bnk][:, gco + 128 * j:gco + 128 * (j + 1)],
              in_=hqb[:, (ku % 2) * 512 + 128 * j:
                      (ku % 2) * 512 + 128 * (j + 1)], identity=eyeb[:])
        hqbfree.append(("sHqT", p.plan["sHqT"]))
        if g % 2 == 1:
            qhalf = quarter % 2
            G = 4 * c + quarter
            if G // 2 >= 1:
                W("vector", *hw_ck[G // 2 - 1])
            W("vector", "sHqT", p.plan["sHqT"])
            O("vector", "tensor_copy", [hqt_st[:]], [pbb[bnk]],
              sem="sHqE",
              out=hqt_st[:, 1024 * qhalf:1024 * (qhalf + 1)],
              in_=pbb[bnk][:, :])
            hqe_ck.append(p.plan["sHqE"])
            if quarter % 2 == 1:
                half = quarter // 2
                W("scalar", "sHqE", hqe_ck[-1])
                hw_ck.append(p.pdma(
                    "scalar", "qhw",
                    hqTh[c, :, 2048 * half:2048 * (half + 1)],
                    hqt_st[:]))
        return rnd_ck["sHq"][ku]

    def emit_r1_load(cc):
        if cc >= 2:
            W("sync", "sHq", r1free_ck[cc - 2])
        W("sync", *r1w_ck[cc])
        r1r_ck.append(p.pdma("sync", "qr1r",
                             r1f[:, (cc % 2) * E:(cc % 2 + 1) * E],
                             r1h[cc * 128:(cc + 1) * 128, :]))

    def emit_head(cc):
        xp = cc % 2
        fc = 8 * xp
        r1c = r1f[:, xp * E:(xp + 1) * E]
        W("vector", *r1r_ck[cc])
        if cc >= 2:
            W("vector", "sSilu", silu_ck[8 * (cc - 2) + 7])
        O("vector", "tensor_reduce", [fcol[:]], [r1f[:]],
          out=fcol[:, fc:fc + 1], in_=r1c, axis=AXL.X, op=ALU.max,
          apply_absolute_value=True)
        O("vector", "tensor_scalar", [fcol[:]], [fcol[:]],
          out=fcol[:, fc + 1:fc + 2], in0=fcol[:, fc:fc + 1], scalar1=1e-5,
          scalar2=None, op0=ALU.max)
        O("vector", "reciprocal", [fcol[:]], [fcol[:]],
          out=fcol[:, fc + 2:fc + 3], in_=fcol[:, fc + 1:fc + 2])
        O("vector", "tensor_scalar", [fcol[:]], [fcol[:]], sem="sXq",
          out=fcol[:, fc + 3:fc + 4], in0=fcol[:, fc + 2:fc + 3],
          scalar1=127.0, scalar2=None, op0=ALU.mult)
        O("vector", "tensor_scalar", [fcol[:]], [fcol[:], mh127[:]],
          out=fcol[:, fc + 4:fc + 5], in0=fcol[:, fc + 1:fc + 2],
          scalar1=mh127[:, 0:1], scalar2=None, op0=ALU.mult)
        O("vector", "tensor_scalar", [fcol[:]], [fcol[:], mh127[:]],
          out=fcol[:, fc + 5:fc + 6], in0=fcol[:, fc + 1:fc + 2],
          scalar1=mh127[:, 1:2], scalar2=None, op0=ALU.mult)
        if cc > 0:
            W("tensor", "sXqE", xqe_ck[cc - 1])
        W("scalar", "sXq", p.plan["sXq"])
        for xh in range(2):
            ku = emit_round(r1c[:, 512 * xh:512 * (xh + 1)],
                            fcol[:, fc + 3:fc + 4], xt1[:])
            if ku >= 2:
                W("vector", *hqbfree[ku - 2])
            W("tensor", "sHq", rnd_ck["sHq"][ku])
            for j in range(4):
                O("tensor", "transpose", [pbb[0]], [hqb[:], eyeb[:]],
                  sem=("sXqT" if j == 3 else None),
                  out=pbb[0][:, 512 * xh + 128 * j:
                             512 * xh + 128 * (j + 1)],
                  in_=hqb[:, (ku % 2) * 512 + 128 * j:
                          (ku % 2) * 512 + 128 * (j + 1)],
                  identity=eyeb[:])
            xqt_ck.append(p.plan["sXqT"])
            hqbfree.append(("sXqT", p.plan["sXqT"]))
        r1free_ck.append(rnd_ck["sHq"][-1])
        W("vector", "sXqT", xqt_ck[-1])
        if cc >= 2:
            W("vector", "sGU", gu_last_ck[cc - 2])
        O("vector", "tensor_copy", [xqt[:]], [pbb[0]], sem="sXqE",
          out=xqt[:, xp * E:(xp + 1) * E], in_=pbb[0][:, :])
        xqe_ck.append(p.plan["sXqE"])

    emit_r1_load(0)
    emit_head(0)
    for c in range(NCH + 1):
        par = c % 2
        if c + 1 < NCH:
            emit_r1_load(c + 1)
        if c > 0:
            hp = (c - 1) % 2
            fc2 = 8 * hp
            if rnd_ck["sF1f"]:
                W("vector", "sF1f", rnd_ck["sF1f"][-1])
            O("vector", "tensor_reduce", [fcol[:]], [hmx8[:]],
              out=fcol[:, fc2 + 6:fc2 + 7], in_=hmx8[:, 8 * hp:8 * hp + 8],
              axis=AXL.X, op=ALU.max)
            O("vector", "tensor_scalar", [mxh_all[:]], [fcol[:]],
              out=mxh_all[:, c - 1:c], in0=fcol[:, fc2 + 6:fc2 + 7],
              scalar1=1e-5, scalar2=None, op0=ALU.max)
            O("vector", "reciprocal", [fcol[:]], [mxh_all[:]],
              out=fcol[:, fc2 + 7:fc2 + 8], in_=mxh_all[:, c - 1:c])
            O("vector", "tensor_scalar", [fcol[:]], [fcol[:]], sem="sXq",
              out=fcol[:, fc2 + 7:fc2 + 8], in0=fcol[:, fc2 + 7:fc2 + 8],
              scalar1=127.0, scalar2=None, op0=ALU.mult)
            W("scalar", "sXq", p.plan["sXq"])
        if c < NCH:
            W("tensor", "sXqE", xqe_ck[c])
        for n in range(NSTR):
            if c < NCH:
                gb, ub = (1, 2) if n % 2 == 0 else (3, 4)
                if c == 0:
                    W("tensor", "sTg", 8 * (n + 1))
                    W("tensor", "sTu", 8 * (n + 1))
                hidx = c * NSTR + n
                if hidx >= 2:
                    W("tensor", "sH", h_ck[hidx - 2])
                    W("tensor", "sSilu", silu_ck[hidx - 2])
                for e in range(ECH):
                    lt = xqt[:, par * E + 128 * e:par * E + 128 * (e + 1)]
                    O("tensor", "ldweights", [], [], weights=lt)
                    O("tensor", "matmul", [pb[gb].ap()], [xqt[:], wg_s[:]],
                      out=pb[gb][:, :], lhsT=lt,
                      rhs=wg3[:, e, 512 * n:512 * (n + 1)],
                      start=(e == 0), stop=(e == ECH - 1))
                    O("tensor", "matmul", [pb[ub].ap()], [xqt[:], wu_s[:]],
                      sem=("sGU" if e == ECH - 1 else None),
                      out=pb[ub][:, :], lhsT=lt,
                      rhs=wu3[:, e, 512 * n:512 * (n + 1)],
                      start=(e == 0), stop=(e == ECH - 1))
            if c > 0:
                emit_hquant_eighth(c - 1, n)
            if n == 0 and c + 1 < NCH:
                emit_head(c + 1)
            if c < NCH:
                spar = n % 2
                fc = 8 * par
                hidx = c * NSTR + n
                W("scalar", "sGU", p.plan["sGU"])
                if hidx >= 2:
                    W("scalar", "sH", h_ck[hidx - 2])
                O("scalar", "activation", [silub[:]],
                  [pb[gb].ap(), fcol[:], zcol[:]], sem="sSilu",
                  out=silub[:, spar * 512:(spar + 1) * 512], in_=pb[gb][:, :],
                  func=AF.Silu, scale=fcol[:, fc + 4:fc + 5], bias=zcol[:, 0:1])
                silu_ck.append(p.plan["sSilu"])
                W("vector", "sSilu", silu_ck[-1])
                O("vector", "scalar_tensor_tensor", [hbuf[:]],
                  [pb[ub].ap(), fcol[:], silub[:]],
                  out=hbuf[:, 512 * n:512 * (n + 1)], in0=pb[ub][:, :],
                  scalar=fcol[:, fc + 5:fc + 6],
                  in1=silub[:, spar * 512:(spar + 1) * 512],
                  op0=ALU.mult, op1=ALU.mult)
                O("vector", "tensor_reduce", [hmx8[:]], [hbuf[:]], sem="sH",
                  out=hmx8[:, 8 * par + n:8 * par + n + 1],
                  in_=hbuf[:, 512 * n:512 * (n + 1)],
                  axis=AXL.X, op=ALU.max, apply_absolute_value=True)
                h_ck.append(p.plan["sH"])
        if c < NCH:
            gu_last_ck.append(p.plan["sGU"])

    p.barrier()

    free("hmx8", "silub", "xt1", "xqt", "hqt_st", "hqb", "rres", "f1h",
         "q32h", "hbuf")
    wd3 = wg_s[:].rearrange("p (c f) -> p c f", c=ICH)
    hqL = wu_s
    rdw = wdTd.rearrange("(c p) f -> p c f", p=128)
    td_ck = []
    ta2_ck = []
    for i in range(ICH):
        for hh in range(2):
            idx = 2 * i + hh
            par_ = idx % 2
            if idx >= 2:
                W("sync", "sTd", td_ck[idx - 2])
            stg = wfst[:, par_ * 512:(par_ + 1) * 512]
            wf_ck.append(p.pdma("sync", "qwf", stg,
                                rdw[:, i, 512 * hh:512 * (hh + 1)]))
            W("scalar", *wf_ck[-1])
            if idx >= 2:
                W("scalar", "sTd", td_ck[idx - 2])
            O("scalar", "activation", [tern_a[:]], [wfst[:], nthb[:], onecol[:]],
              out=tern_a[:, par_ * 512:(par_ + 1) * 512], in_=stg,
              func=AF.Sign, bias=nthb[:, 2:3], scale=onecol[:, 0:1])
            O("scalar", "activation", [tern_b[:]], [wfst[:], thb[:], onecol[:]],
              sem="sTa",
              out=tern_b[:, par_ * 512:(par_ + 1) * 512], in_=stg,
              func=AF.Sign, bias=thb[:, 2:3], scale=onecol[:, 0:1])
            ta2_ck.append(p.plan["sTa"])
            W("vector", "sTa", ta2_ck[-1])
            O("vector", "tensor_tensor", [wg_s[:]], [tern_a[:], tern_b[:]],
              sem="sTd",
              out=wd3[:, i, 512 * hh:512 * (hh + 1)],
              in0=tern_a[:, par_ * 512:(par_ + 1) * 512],
              in1=tern_b[:, par_ * 512:(par_ + 1) * 512], op=ALU.add)
            td_ck.append(p.plan["sTd"])

    dn_ck, ep_ck, epr_ck, hr_ck, r1r2_ck, qo_ck = [], [], [], [], [], []
    W("sync", "qhw0", p.plan["qhw0"])
    W("sync", "qhw1", p.plan["qhw1"])
    W("sync", "qr1w0", p.plan["qr1w0"])
    W("sync", "qr1w1", p.plan["qr1w1"])
    for c in range(NCH):
        par = c % 2
        hqc = hqL[:, par * INNER:(par + 1) * INNER]
        if c >= 2:
            W("sync", "sDn", dn_ck[c - 2])
        hr_ck.append(p.pdma("sync", "qhr", hqc, hqTh[c, :, :]))
        if c >= 2:
            W("sync", "sEp", epr_ck[c - 2])
        r1r2_ck.append(p.pdma("sync", "qr1r",
                              r1f[:, par * E:(par + 1) * E],
                              r1h[c * 128:(c + 1) * 128, :]))
        W("tensor", *hr_ck[c])
        db0, db1 = (1, 2) if par == 0 else (3, 4)
        if c > 1:
            W("tensor", "sEp", epr_ck[c - 2])
        for i in range(ICH):
            if c == 0:
                W("tensor", "sTd", td_ck[2 * i + 1])
            lt = hqc[:, 128 * i:128 * (i + 1)]
            O("tensor", "ldweights", [], [], weights=lt)
            O("tensor", "matmul", [pb[db0].ap()], [hqL[:], wg_s[:]],
              out=pb[db0][:, :], lhsT=lt, rhs=wd3[:, i, 0:512],
              start=(i == 0), stop=(i == ICH - 1))
            O("tensor", "matmul", [pb[db1].ap()], [hqL[:], wg_s[:]],
              sem=("sDn" if i == ICH - 1 else None),
              out=pb[db1][:, :], lhsT=lt, rhs=wd3[:, i, 512:1024],
              start=(i == 0), stop=(i == ICH - 1))
        dn_ck.append(p.plan["sDn"])
        r1c = r1f[:, par * E:(par + 1) * E]
        W("vector", *r1r2_ck[c])
        W("vector", "sDn", dn_ck[c])
        if c > 0:
            W("vector", "sEp", ep_ck[c - 1])
        O("vector", "memset", [fcol[:]], [], ap=fcol[:, 1:3], constant=0.0)
        O("vector", "memset", [fcol[:]], [], ap=fcol[:, 4:5], constant=0.0)
        O("vector", "tensor_scalar", [fcol[:]], [mxh_all[:], mh127[:]],
          out=fcol[:, 0:1], in0=mxh_all[:, c:c + 1], scalar1=mh127[:, 2:3],
          scalar2=None, op0=ALU.mult)
        O("vector", "scalar_tensor_tensor", [r0b[:], fcol[:]],
          [pb[db0].ap(), r1f[:]],
          out=r0b[:, 0:512], in0=pb[db0][:, :], scalar=fcol[:, 0:1],
          in1=r1c[:, 0:512], op0=ALU.mult, op1=ALU.add,
          accum_out=fcol[:, 1:2])
        O("vector", "scalar_tensor_tensor", [r0b[:], fcol[:]],
          [pb[db1].ap(), r1f[:]], sem="sEp",
          out=r0b[:, 512:1024], in0=pb[db1][:, :], scalar=fcol[:, 0:1],
          in1=r1c[:, 512:1024], op0=ALU.mult, op1=ALU.add,
          accum_out=fcol[:, 2:3])
        epr_ck.append(p.plan["sEp"])
        O("vector", "tensor_scalar", [fcol[:]], [fcol[:]],
          out=fcol[:, 3:4], in0=fcol[:, 1:2], scalar1=fcol[:, 2:3],
          scalar2=None, op0=ALU.add)
        W("scalar", "sEp", p.plan["sEp"])
        O("scalar", "activation", [wfst[:], fcol[:]], [r0b[:]], sem="sEp",
          out=wfst[:], in_=r0b[:], func=AF.Square, accum_out=fcol[:, 4:5])
        W("vector", "sEp", p.plan["sEp"])
        O("vector", "tensor_scalar", [fcol[:]], [fcol[:]],
          out=fcol[:, 5:6], in0=fcol[:, 3:4], scalar1=fcol[:, 3:4],
          scalar2=-1.0 / (n_ln * n_ln), op0=ALU.mult, op1=ALU.mult)
        O("vector", "tensor_scalar", [fcol[:]], [fcol[:]],
          out=fcol[:, 5:6], in0=fcol[:, 5:6], scalar1=EPS_LN, scalar2=None,
          op0=ALU.add)
        O("vector", "tensor_scalar", [fcol[:]], [fcol[:]], sem="sEp",
          out=fcol[:, 6:7], in0=fcol[:, 4:5], scalar1=1.0 / n_ln,
          scalar2=fcol[:, 5:6], op0=ALU.mult, op1=ALU.add)
        W("scalar", "sEp", p.plan["sEp"])
        O("scalar", "activation", [fcol[:]], [fcol[:]], sem="sEp",
          out=fcol[:, 6:7], in_=fcol[:, 6:7], func=AF.Sqrt)
        W("vector", "sEp", p.plan["sEp"])
        O("vector", "reciprocal", [fcol[:]], [fcol[:]],
          out=fcol[:, 7:8], in_=fcol[:, 6:7])
        O("vector", "tensor_scalar", [fcol[:]], [fcol[:]], sem="sEp",
          out=fcol[:, 5:6], in0=fcol[:, 3:4], scalar1=-1.0 / n_ln,
          scalar2=fcol[:, 7:8], op0=ALU.mult, op1=ALU.mult)
        if c >= 2:
            W("vector", *qo_ck[c - 2])
        O("vector", "tensor_scalar", [r1b[:]], [r0b[:], fcol[:]], sem="sEp",
          out=r1b[:, par * E:(par + 1) * E], in0=r0b[:],
          scalar1=fcol[:, 7:8], scalar2=fcol[:, 5:6],
          op0=ALU.mult, op1=ALU.add)
        ep_ck.append(p.plan["sEp"])
        W("scalar", "sEp", ep_ck[c])
        qo_ck.append(p.pdma("scalar", "qo", out_d[c * 128:(c + 1) * 128, :],
                            r1b[:, par * E:(par + 1) * E]))

    if DBG:
        W("sync", "qo0", p.plan["qo0"])
        W("sync", "qo1", p.plan["qo1"])
        DMA("sync", dbg_r1, r1h, "qcc")
        DMA("sync", dbg_sf, cc_out.ap(), "qcc")
        DMA("sync", dbg_mh, mhb[:], "qcc")
        DMA("sync", dbg_hq, hqTh[0, :, :], "qcc")
        W("sync", "qcc", p.plan["qcc"])
    W("sync", "qo0", p.plan["qo0"])
    W("sync", "qo1", p.plan["qo1"])

    with nc.Block() as block:
        @block.sync
        def _(eng):
            for f in p.q["sync"]:
                f(eng)

        @block.scalar
        def _(eng):
            for f in p.q["scalar"]:
                f(eng)

        @block.vector
        def _(eng):
            for f in p.q["vector"]:
                f(eng)

        @block.tensor
        def _(eng):
            for f in p.q["tensor"]:
                f(eng)

        @block.gpsimd
        def _(eng):
            for f in p.q["gpsimd"]:
                f(eng)
    return nc, p


def kernel(L0_attn_query, L0_attn_Wq, L0_attn_bq, L0_attn_Wk, L0_attn_bk,
           L0_attn_Wv, L0_attn_bv, L0_attn_Wo, L0_attn_bo,
           L0_ffn_w_gate, L0_ffn_w_up, L0_ffn_w_down):
    x = np.ascontiguousarray(
        np.asarray(L0_attn_query, dtype=np.float32).reshape(B * L, E))
    asf = lambda a: np.ascontiguousarray(np.asarray(a, dtype=np.float32))
    Wq, Wk, Wv, Wo = map(asf, (L0_attn_Wq, L0_attn_Wk, L0_attn_Wv, L0_attn_Wo))
    bq, bk, bv, bo = map(asf, (L0_attn_bq, L0_attn_bk, L0_attn_bv, L0_attn_bo))
    wg, wu, wd = map(asf, (L0_ffn_w_gate, L0_ffn_w_up, L0_ffn_w_down))

    common = {
        "wqT": np.ascontiguousarray(Wq.T), "wkT": np.ascontiguousarray(Wk.T),
        "wvT": np.ascontiguousarray(Wv.T), "woT": np.ascontiguousarray(Wo.T),
        "bqpp": np.ascontiguousarray(bq.reshape(ECH, 128).T),
        "bkb": np.ascontiguousarray(np.broadcast_to(bk, (128, E))),
        "bob": np.ascontiguousarray(np.broadcast_to(bo, (128, E))),
        "bv2": np.ascontiguousarray(
            bv.reshape(NPAIR, 2, D).transpose(1, 0, 2).reshape(2, 512)),
        "wgT": np.ascontiguousarray(wg.T), "wuT": np.ascontiguousarray(wu.T),
        "wdT": np.ascontiguousarray(wd.T),
        "eyeb": np.eye(128, dtype=ml_dtypes.bfloat16),
        "eyef": np.eye(128, dtype=np.float32),
    }
    in_maps = []
    for c in range(NCORES):
        xc = x[c * TPC:(c + 1) * TPC, :]
        m = dict(common)
        m["x"] = np.ascontiguousarray(xc)
        m["xT"] = np.ascontiguousarray(xc.T)
        in_maps.append(m)

    nc, _ = build_full()
    res = run_bass_kernel_spmd(nc, in_maps, core_ids=list(range(NCORES)))
    outs = [np.asarray(res.results[c]["out"], dtype=np.float32)
            for c in range(NCORES)]
    return np.concatenate(outs, axis=0).reshape(B, L, E)


# version 16 (final)
# speedup vs baseline: 1.9222x; 1.2210x over previous
import numpy as np
import ml_dtypes
import concourse.bass as bass
import concourse.mybir as mybir
from concourse.bass_utils import run_bass_kernel_spmd

F32 = mybir.dt.float32
BF16 = mybir.dt.bfloat16
I32 = mybir.dt.int32
ALU = mybir.AluOpType
AF = mybir.ActivationFunctionType
AXL = mybir.AxisListType

B, L, E, H, D = 4, 8192, 1024, 16, 64
NCORES = 8
TPC = L * B // NCORES
NCH = TPC // 128
NSC = TPC // 512
ECH = E // 128
INNER = 4096
NSTR = INNER // 512
ICH = INNER // 128
NPAIR = 8
EPS_ATTN = 1e-6
EPS_LN = 1e-5
WCNT = float(INNER * E)

ENGINES = ("sync", "scalar", "vector", "tensor", "gpsimd")
DBG = False


class Prog:

    def __init__(self, nc):
        self.nc = nc
        self.q = {e: [] for e in ENGINES}
        self.sems = {}
        self.plan = {}
        self.wr = {e: set() for e in ENGINES}
        self.rd = {e: set() for e in ENGINES}
        self.dmak = {}

    def sem(self, name):
        h = self.nc.semaphore(name).__enter__()
        self.sems[name] = h
        self.plan[name] = 0

    @staticmethod
    def _names(aps):
        out = set()
        for a in aps:
            t = getattr(a, "tensor", None)
            if t is not None:
                out.add(t.name)
        return out

    def wait(self, eng, sem, value=None):
        if value is None:
            value = self.plan[sem]
        if value <= 0:
            return
        h = self.sems[sem]
        self.q[eng].append(lambda e, h=h, v=value: e.wait_ge(h, v))

    def op(self, eng, method, outs, ins, sem=None, **kw):
        onames = self._names(outs)
        inames = self._names(ins)
        if eng != "tensor" and (
                (inames & self.wr[eng]) or
                (onames & (self.wr[eng] | self.rd[eng]))):
            self.q[eng].append(lambda e: e.drain())
            self.wr[eng].clear()
            self.rd[eng].clear()
        self.wr[eng] |= onames
        self.rd[eng] |= inames
        semh = self.sems[sem] if sem else None
        if sem:
            self.plan[sem] += 1

        def f(e, method=method, kw=kw, semh=semh):
            inst = getattr(e, method)(**kw)
            if semh is not None:
                inst.then_inc(semh, 1)
        self.q[eng].append(f)

    def dma(self, eng, out, in_, sem):
        semh = self.sems[sem]
        self.plan[sem] += 16

        def f(e, out=out, in_=in_, semh=semh):
            e.dma_start(out=out, in_=in_).then_inc(semh, 16)
        self.q[eng].append(f)

    def pdma(self, eng, stream, out, in_):
        k = self.dmak.get(stream, 0)
        self.dmak[stream] = k + 1
        sem = f"{stream}{k % 2}"
        if k >= 2:
            self.wait(eng, sem, 16 * (k // 2))
        self.dma(eng, out, in_, sem)
        return (sem, 16 * (k // 2 + 1))

    def raw(self, eng, fn):
        self.q[eng].append(fn)

    def barrier(self, name="bar"):
        h = self.sems[name]
        self.plan[name] += 5
        v = self.plan[name]
        for eng in ENGINES:
            def f(e, h=h):
                e.drain()
                e.nop().then_inc(h, 1)
            self.q[eng].append(f)
            self.wr[eng].clear()
            self.rd[eng].clear()
        for eng in ENGINES:
            self.q[eng].append(lambda e, h=h, v=v: e.wait_ge(h, v))


def build_full():
    nc = bass.Bass("TRN2", target_bir_lowering=False, debug=False)
    p = Prog(nc)
    W, O, DMA = p.wait, p.op, p.dma

    ap = lambda t: t.ap()
    xT = ap(nc.dram_tensor("xT", [E, TPC], F32, kind="ExternalInput"))
    x_tm = ap(nc.dram_tensor("x", [TPC, E], F32, kind="ExternalInput"))
    wqT = ap(nc.dram_tensor("wqT", [E, E], F32, kind="ExternalInput"))
    wkT = ap(nc.dram_tensor("wkT", [E, E], F32, kind="ExternalInput"))
    wvT = ap(nc.dram_tensor("wvT", [E, E], F32, kind="ExternalInput"))
    woT = ap(nc.dram_tensor("woT", [E, E], F32, kind="ExternalInput"))
    bqpp = ap(nc.dram_tensor("bqpp", [128, ECH], F32, kind="ExternalInput"))
    bkb_d = ap(nc.dram_tensor("bkb", [128, E], F32, kind="ExternalInput"))
    bob_d = ap(nc.dram_tensor("bob", [128, E], F32, kind="ExternalInput"))
    bv2_d = ap(nc.dram_tensor("bv2", [2, 512], F32, kind="ExternalInput"))
    wgTd = ap(nc.dram_tensor("wgT", [E, INNER], F32, kind="ExternalInput"))
    wuTd = ap(nc.dram_tensor("wuT", [E, INNER], F32, kind="ExternalInput"))
    wdTd = ap(nc.dram_tensor("wdT", [INNER, E], F32, kind="ExternalInput"))
    eyeb_d = ap(nc.dram_tensor("eyeb", [128, 128], BF16, kind="ExternalInput"))
    eyef_d = ap(nc.dram_tensor("eyef", [128, 128], F32, kind="ExternalInput"))
    out_d = ap(nc.dram_tensor("out", [TPC, E], F32, kind="ExternalOutput"))

    if DBG:
        dbg_r1 = ap(nc.dram_tensor("dbg_r1", [TPC, E], F32,
                                   kind="ExternalOutput"))
        dbg_sf = ap(nc.dram_tensor("dbg_sf", [128, NPAIR * 132], F32,
                                   kind="ExternalOutput"))
        dbg_mh = ap(nc.dram_tensor("dbg_mh", [128, 3], F32,
                                   kind="ExternalOutput"))
        dbg_hq = ap(nc.dram_tensor("dbg_hq", [128, INNER], BF16,
                                   kind="ExternalOutput"))
    cc_in = nc.dram_tensor("cc_in", [128, NPAIR * 132], F32)
    cc_out = nc.dram_tensor("cc_out", [128, NPAIR * 132], F32)
    r1h = ap(nc.dram_tensor("r1h", [TPC, E], F32))
    wgt_h = ap(nc.dram_tensor("wgth", [E, INNER], BF16))
    wut_h = ap(nc.dram_tensor("wuth", [E, INNER], BF16))
    hqTh = ap(nc.dram_tensor("hqTh", [NCH, 128, INNER], BF16))

    live = {}

    def alloc(name, shape, dtype, side="left", psum=False):
        if psum:
            cm = nc.psum_tensor(name, shape, dtype)
        else:
            cm = nc.sbuf_tensor(name, shape, dtype, side=side)
        t = cm.__enter__()
        live[name] = cm
        return t

    def free(*names):
        for n in names:
            live.pop(n).__exit__(None, None, None)

    for s in ("qconst", "qwa", "qcc", "qccr", "ccd",
              "qxt0", "qxt1", "qws0", "qws1", "qx0", "qx1", "qr1w0", "qr1w1",
              "qwf20", "qwf21",
              "qr1r0", "qr1r1", "qwf0", "qwf1", "qhw0", "qhw1", "qhr0",
              "qtw0", "qtw1", "qwl0", "qwl1",
              "qhr1", "qo0", "qo1",
              "sKV", "sPhiD", "sPhiK", "sER", "sVc", "sS", "sWred",
              "sAf", "sQmm", "sER2", "sPhiQ", "sSb",
              "sNum", "sDiv", "sCtxT", "sCtxE", "sWo", "sXpb", "sR0", "sR1",
              "sTg", "sTu", "sTd",
              "sXq", "sXqT", "sXqE", "sGU", "sSilu", "sH", "sHq", "sHqT",
              "sHqE", "sDn", "sEp", "sF1f", "sRr", "sCd", "sTa", "bar"):
        p.sem(s)

    eyeb = alloc("eyeb_s", [128, 128], BF16)
    eyef = alloc("eyef_s", [128, 128], F32)
    bq_s = alloc("bq_s", [128, ECH], F32)
    onesb = alloc("onesb", [128, 1], BF16)
    onesr = alloc("onesr", [1, 128], F32)
    zcol = alloc("zcol", [128, 1], F32)
    onecol = alloc("onecol", [128, 1], F32)
    mhb = alloc("mhb", [128, 3], F32)
    thb = alloc("thb", [128, 3], F32)
    nthb = alloc("nthb", [128, 3], F32)
    mh127 = alloc("mh127", [128, 3], F32)
    mxh_all = alloc("mxh_all", [128, NCH], F32)
    tcol = alloc("tcol", [128, 16], F32)
    fcol = alloc("fcol", [128, 16], F32)
    r0b = alloc("r0b", [128, E], F32)
    r1b = alloc("r1b", [128, 2 * E], F32)

    class PV:
        def __init__(self, a):
            self._a = a

        def ap(self):
            return self._a

        def __getitem__(self, k):
            return self._a[k]

    pbig = alloc("pbig", [128, 4096], F32, psum=True)
    pb = [PV(pbig[:, 512 * k:512 * (k + 1)]) for k in range(8)]
    pbbig = pbig.ap().bitcast(BF16)
    pbb = [pbbig[:, 1024 * k:1024 * (k + 1)] for k in range(8)]

    DMA("sync", eyeb[:], eyeb_d, "qconst")
    DMA("sync", eyef[:], eyef_d, "qconst")
    DMA("sync", bq_s[:], bqpp, "qconst")
    O("vector", "memset", [onesb[:]], [], sem="sSb", ap=onesb[:], constant=1.0)
    O("vector", "memset", [onesr[:]], [], sem="sSb", ap=onesr[:], constant=1.0)
    O("vector", "memset", [zcol[:]], [], sem="sSb", ap=zcol[:], constant=0.0)
    O("vector", "memset", [onecol[:]], [], sem="sSb", ap=onecol[:], constant=1.0)

    bkb = alloc("bkb_s", [128, E], F32)
    bob = alloc("bob_s", [128, E], F32)
    bv2 = alloc("bv2_s", [2, 512], F32)
    ccs = alloc("ccs", [128, NPAIR * 132], F32)
    sfull = alloc("sfullb", [128, NPAIR * 130], BF16)
    wparts = alloc("wparts", [128, 48], F32)
    xqbuf = alloc("xqbuf", [128, ECH * TPC], BF16)

    phik = alloc("phik", [128, 2 * E], BF16, side="right")
    vbuf = alloc("vbuf", [128, 2 * E], BF16, side="right")
    kqbuf = alloc("kqbuf", [128, E], F32, side="right")
    ebuf = alloc("ebuf", [128, E], F32, side="right")
    rbuf = alloc("rbuf", [128, E], F32, side="right")
    wk_s = alloc("wk_s", [128, ECH * E], BF16, side="right")
    wv_s = alloc("wv_s", [128, ECH * E], BF16, side="right")
    wstage = alloc("wstage", [128, 2 * 2048], F32, side="right")

    DMA("sync", bkb[:], bkb_d, "qconst")
    DMA("sync", bob[:], bob_d, "qconst")
    DMA("sync", bv2[:], bv2_d, "qconst")
    O("vector", "memset", [sfull[:]], [], sem="sSb", ap=sfull[:], constant=0.0)
    O("vector", "memset", [ccs[:]], [], sem="sSb", ap=ccs[:], constant=0.0)
    for k in range(3):
        O("vector", "memset", [pb[k].ap()], [], sem="sSb",
          ap=pb[k][:, :], constant=0.0)
    for _e in ("tensor", "vector", "scalar", "gpsimd"):
        W(_e, "qconst", p.plan["qconst"])
        W(_e, "sSb", p.plan["sSb"])

    for wsrc, wdst in ((wkT, wk_s), (wvT, wv_s)):
        DMA("gpsimd", wdst[:].rearrange("p (c f) -> p c f", c=ECH),
            wsrc.rearrange("(c p) f -> p c f", p=128), "qwa")
    xr = xT.rearrange("(c p) t -> p c t", p=128)
    xq3 = xqbuf[:].rearrange("p (c t) -> p c t", c=ECH)
    xt_ck = []
    for sc in range(NSC):
        xt_ck.append(p.pdma("gpsimd", "qxt",
                            xq3[:, :, sc * 512:(sc + 1) * 512],
                            xr[:, :, sc * 512:(sc + 1) * 512]))

    wpieces = []
    for wd_ in (wgTd, wuTd, wdTd):
        nrowch = wd_.shape[0] // 128
        ncol = wd_.shape[1]
        r = wd_.rearrange("(c p) f -> p c f", p=128)
        if ncol >= 2048:
            for i in range(nrowch):
                for j in range(ncol // 2048):
                    wpieces.append(r[:, i:i + 1, 2048 * j:2048 * (j + 1)])
        else:
            per = 2048 // ncol
            for i in range(0, nrowch, per):
                wpieces.append(r[:, i:i + per, :])
    assert len(wpieces) == 48
    ws_ck = []

    wk3 = wk_s[:].rearrange("p (c f) -> p c f", c=ECH)
    wv3 = wv_s[:].rearrange("p (c f) -> p c f", c=ECH)
    sbank = [0, 0, 0, 1, 1, 1, 2, 2]
    scol = [0, 132, 264, 0, 132, 264, 0, 132]

    def emit_s_mms(tc):
        par = tc % 2
        W("tensor", "sPhiK")
        W("tensor", "sVc")
        for pr in range(NPAIR):
            bk_, co = sbank[pr], scol[pr]
            lt = phik[:, par * E + 128 * pr: par * E + 128 * (pr + 1)]
            O("tensor", "ldweights", [], [], weights=lt)
            O("tensor", "matmul", [pb[bk_].ap()], [],
              out=pb[bk_][:, co:co + 128], lhsT=lt,
              rhs=vbuf[:, par * E + 128 * pr: par * E + 128 * (pr + 1)],
              start=(tc == 0 and pr % 3 == 0), stop=False,
              skip_group_check=True)
            O("tensor", "matmul", [pb[bk_].ap()], [],
              sem=("sS" if pr == NPAIR - 1 else None),
              out=pb[bk_][:, co + 128:co + 129], lhsT=lt, rhs=onesb[:],
              start=False, stop=(tc == NCH - 1), skip_group_check=True)

    for tc in range(NCH):
        par = tc % 2
        sc = tc // 4
        W("tensor", *xt_ck[sc])
        if tc > 0:
            W("tensor", "sPhiD", tc)
            W("tensor", "sVc", tc)
        if tc == 0:
            W("tensor", "qwa", 32)
        for e in range(ECH):
            lt = xq3[:, e, tc * 128:(tc + 1) * 128]
            O("tensor", "ldweights", [], [], weights=lt)
            for half, bnk, w3 in ((0, 3, wk3), (1, 4, wk3),
                                  (0, 5, wv3), (1, 6, wv3)):
                O("tensor", "matmul", [pb[bnk].ap()], [],
                  sem=("sKV" if (e == ECH - 1 and bnk == 6) else None),
                  out=pb[bnk][:, :], lhsT=lt,
                  rhs=w3[:, e, half * 512:(half + 1) * 512],
                  start=(e == 0), stop=(e == ECH - 1))
        if tc > 0:
            emit_s_mms(tc - 1)

        if tc >= 1:
            W("vector", "sER")
        W("vector", "sKV", tc + 1)
        O("vector", "tensor_tensor", [kqbuf[:]], [pb[3].ap(), bkb[:]],
          out=kqbuf[:, 0:512], in0=pb[3][:, :], in1=bkb[:, 0:512], op=ALU.add)
        O("vector", "tensor_tensor", [kqbuf[:]], [pb[4].ap(), bkb[:]],
          sem="sPhiD",
          out=kqbuf[:, 512:1024], in0=pb[4][:, :], in1=bkb[:, 512:1024],
          op=ALU.add)
        if tc >= 1:
            W("scalar", "sPhiK")
        W("scalar", "sPhiD", p.plan["sPhiD"])
        O("scalar", "activation", [ebuf[:]], [kqbuf[:]],
          out=ebuf[:], in_=kqbuf[:], func=AF.Exp)
        O("scalar", "activation", [rbuf[:]], [kqbuf[:]], sem="sER",
          out=rbuf[:], in_=kqbuf[:], func=AF.Relu)
        if tc >= 2:
            W("vector", "sS", tc - 1)
        W("vector", "sER", p.plan["sER"])
        O("vector", "scalar_tensor_tensor", [phik[:]], [ebuf[:], rbuf[:]],
          sem="sPhiK",
          out=phik[:, par * E:(par + 1) * E], in0=ebuf[:], scalar=1.0,
          in1=rbuf[:], op0=ALU.min, op1=ALU.add)
        W("scalar", "sKV", p.plan["sKV"])
        if tc >= 2:
            W("scalar", "sS", tc - 1)
        O("scalar", "copy", [vbuf[:]], [pb[5].ap()],
          out=vbuf[:, par * E:par * E + 512], in_=pb[5][:, :])
        O("scalar", "copy", [vbuf[:]], [pb[6].ap()], sem="sVc",
          out=vbuf[:, par * E + 512:(par + 1) * E], in_=pb[6][:, :])

        idxs = [i for i in range(48) if i * 32 // 48 == tc]
        for idx in idxs:
            bufp = idx % 2
            if idx >= 2:
                W("sync", "sWred", idx - 1)
            ws_ck.append(p.pdma("sync", "qws",
                                wstage[:, bufp * 2048:(bufp + 1) * 2048],
                                wpieces[idx]))
            W("vector", *ws_ck[idx])
            O("vector", "tensor_reduce", [wparts[:]], [wstage[:]], sem="sWred",
              out=wparts[:, idx:idx + 1],
              in_=wstage[:, bufp * 2048:(bufp + 1) * 2048],
              axis=AXL.X, op=ALU.add, apply_absolute_value=True)

    emit_s_mms(NCH - 1)

    W("vector", "sWred", p.plan["sWred"])
    for w in range(3):
        O("vector", "tensor_reduce", [tcol[:]], [wparts[:]],
          sem=("sAf" if w == 2 else None),
          out=tcol[:, w:w + 1], in_=wparts[:, 16 * w:16 * (w + 1)],
          axis=AXL.X, op=ALU.add)
    W("tensor", "sAf", p.plan["sAf"])
    W("tensor", "sS", p.plan["sS"])
    O("tensor", "transpose", [pb[7].ap()], [tcol[:], eyef[:]], sem="sAf",
      out=pb[7][0:3, 0:128], in_=tcol[:, 0:3], identity=eyef[:])
    W("vector", "sAf", p.plan["sAf"])
    O("vector", "tensor_reduce", [tcol[:]], [pb[7].ap()],
      out=tcol[0:3, 3:4], in_=pb[7][0:3, 0:128], axis=AXL.X, op=ALU.add)
    O("vector", "tensor_scalar", [tcol[:]], [tcol[:]], sem="sAf",
      out=tcol[0:3, 4:5], in0=tcol[0:3, 3:4], scalar1=1.0 / WCNT,
      scalar2=1e-5, op0=ALU.mult, op1=ALU.max)
    W("tensor", "sAf", p.plan["sAf"])
    O("tensor", "transpose", [pb[6].ap()], [tcol[:], eyef[:]], sem="sAf",
      out=pb[6][0:1, 0:3], in_=tcol[0:3, 4:5], identity=eyef[0:3, 0:3])
    W("vector", "sAf", p.plan["sAf"])
    O("vector", "tensor_copy", [tcol[:]], [pb[6].ap()], sem="sAf",
      out=tcol[0:1, 5:8], in_=pb[6][0:1, 0:3])
    W("tensor", "sAf", p.plan["sAf"])
    O("tensor", "matmul", [pb[5].ap()], [onesr[:], tcol[:]], sem="sAf",
      out=pb[5][:, 0:3], lhsT=onesr[:], rhs=tcol[0:1, 5:8],
      start=True, stop=True)
    W("vector", "sAf", p.plan["sAf"])
    O("vector", "tensor_copy", [mhb[:]], [pb[5].ap()],
      out=mhb[:], in_=pb[5][:, 0:3])
    O("vector", "tensor_scalar", [thb[:]], [mhb[:]],
      out=thb[:], in0=mhb[:], scalar1=0.5, scalar2=None, op0=ALU.mult)
    O("vector", "tensor_scalar", [nthb[:]], [mhb[:]],
      out=nthb[:], in0=mhb[:], scalar1=-0.5, scalar2=None, op0=ALU.mult)
    O("vector", "tensor_scalar", [mh127[:]], [mhb[:]], sem="sAf",
      out=mh127[:], in0=mhb[:], scalar1=1.0 / 254.0, scalar2=None,
      op0=ALU.mult)

    W("vector", "sS", p.plan["sS"])
    for k in range(3):
        ncols = 396 if k < 2 else 264
        O("vector", "tensor_copy", [ccs[:]], [pb[k].ap()],
          sem=("sSb" if k == 2 else None),
          out=ccs[:, 396 * k:396 * k + ncols], in_=pb[k][:, 0:ncols])
    W("sync", "sSb", p.plan["sSb"])
    DMA("sync", cc_in.ap(), ccs[:], "qcc")
    free("wstage", "wv_s", "wk_s")
    phiqt = alloc("phiqt", [128, ECH * TPC], BF16)
    wq_s = alloc("wq_s", [128, ECH * E], BF16, side="right")
    wq3 = wq_s[:].rearrange("p (c f) -> p c f", c=ECH)
    W("gpsimd", "qcc", p.plan["qcc"])

    def _cc(e):
        e.collective_compute(
            "AllReduce", ALU.add,
            replica_groups=[[0, 1], [2, 3], [4, 5], [6, 7]],
            ins=[cc_in.ap().opt()], outs=[cc_out.ap().opt()],
        ).then_inc(p.sems["ccd"], 1)
    p.raw("gpsimd", _cc)
    p.plan["ccd"] += 1
    W("gpsimd", "sS", p.plan["sS"])
    W("gpsimd", "sVc", p.plan["sVc"])
    DMA("gpsimd", wq_s[:].rearrange("p (c f) -> p c f", c=ECH),
        wqT.rearrange("(c p) f -> p c f", p=128), "qwa")
    W("gpsimd", "ccd", p.plan["ccd"])
    DMA("gpsimd", ccs[:], cc_out.ap(), "qccr")

    pq3 = phiqt[:].rearrange("p (c t) -> p c t", c=ECH)

    qmm_ck, phiq_ck, er2_ck = [], [], []
    W("tensor", "qwa", p.plan["qwa"])
    W("tensor", "sSb", p.plan["sSb"])
    W("tensor", "sAf", p.plan["sAf"])
    for sc in range(NSC):
        for f in range(ECH):
            i = sc * ECH + f
            if sc > 0:
                W("tensor", "sPhiQ", phiq_ck[(sc - 1) * ECH + f])
            for e in range(ECH):
                lt = wq3[:, e, 128 * f:128 * (f + 1)]
                O("tensor", "ldweights", [], [], weights=lt)
                O("tensor", "matmul", [pb[f].ap()], [],
                  sem=("sQmm" if e == ECH - 1 else None),
                  out=pb[f][:, :], lhsT=lt,
                  rhs=xq3[:, e, sc * 512:(sc + 1) * 512],
                  start=(e == 0), stop=(e == ECH - 1))
            qmm_ck.append(p.plan["sQmm"])
            par = f % 2
            if i >= 2:
                W("scalar", "sPhiQ", phiq_ck[i - 2])
            W("scalar", "sQmm", qmm_ck[i])
            O("scalar", "activation", [ebuf[:]], [pb[f].ap(), bq_s[:], onecol[:]],
              out=ebuf[:, par * 512:(par + 1) * 512], in_=pb[f][:, :],
              func=AF.Exp, bias=bq_s[:, f:f + 1], scale=onecol[:, 0:1])
            O("scalar", "activation", [rbuf[:]], [pb[f].ap(), bq_s[:], onecol[:]],
              sem="sER2",
              out=rbuf[:, par * 512:(par + 1) * 512], in_=pb[f][:, :],
              func=AF.Relu, bias=bq_s[:, f:f + 1], scale=onecol[:, 0:1])
            er2_ck.append(p.plan["sER2"])
            W("vector", "sER2", er2_ck[i])
            O("vector", "scalar_tensor_tensor", [pq3], [ebuf[:], rbuf[:]],
              sem="sPhiQ",
              out=pq3[:, f, sc * 512:(sc + 1) * 512],
              in0=ebuf[:, par * 512:(par + 1) * 512], scalar=1.0,
              in1=rbuf[:, par * 512:(par + 1) * 512],
              op0=ALU.min, op1=ALU.add)
            phiq_ck.append(p.plan["sPhiQ"])

    zb2a = alloc("zb2a", [128, 16], F32, side="right")
    zall = alloc("zall", [2, 1024], F32, side="right")
    W("vector", "qccr", p.plan["qccr"])
    W("tensor", "sPhiQ", p.plan["sPhiQ"])
    O("vector", "memset", [zb2a[:]], [], ap=zb2a[:], constant=0.0)
    for pr in range(NPAIR):
        base = 132 * pr
        O("vector", "tensor_copy", [zb2a[:]], [ccs[:]],
          out=zb2a[0:64, 2 * pr:2 * pr + 1],
          in_=ccs[0:64, base + 128:base + 129])
        O("vector", "tensor_copy", [zb2a[:]], [ccs[:]],
          sem=("sSb" if pr == NPAIR - 1 else None),
          out=zb2a[64:128, 2 * pr + 1:2 * pr + 2],
          in_=ccs[64:128, base + 128:base + 129])
    W("tensor", "sSb", p.plan["sSb"])
    for pr in range(NPAIR):
        bnk = 7 if pr < 4 else 3
        O("tensor", "transpose", [pb[bnk].ap()], [zb2a[:], eyef[:]],
          sem=("sSb" if pr % 4 == 3 else None),
          out=pb[bnk][0:2, 128 * (pr % 4):128 * (pr % 4) + 128],
          in_=zb2a[:, 2 * pr:2 * pr + 2], identity=eyef[:])
    W("vector", "sSb", p.plan["sSb"])
    O("vector", "tensor_copy", [zall[:]], [pb[7].ap()],
      out=zall[:, 0:512], in_=pb[7][0:2, 0:512])
    O("vector", "tensor_copy", [zall[:]], [pb[3].ap()], sem="sSb",
      out=zall[:, 512:1024], in_=pb[3][0:2, 0:512])
    W("tensor", "sSb", p.plan["sSb"])
    for pr in range(NPAIR):
        O("tensor", "matmul", [pb[6].ap()], [zall[:], bv2[:]],
          sem=("sSb" if pr == NPAIR - 1 else None),
          out=pb[6][:, 64 * pr:64 * (pr + 1)],
          lhsT=zall[:, 128 * pr:128 * pr + 128],
          rhs=bv2[:, 64 * pr:64 * (pr + 1)],
          start=(pr == 0), stop=(pr == NPAIR - 1), skip_group_check=True)
    W("vector", "sSb", p.plan["sSb"])
    for pr in range(NPAIR):
        base = 132 * pr
        sb_ = 130 * pr
        O("vector", "tensor_tensor", [sfull[:]], [ccs[:], pb[6].ap()],
          out=sfull[0:64, sb_:sb_ + 64], in0=ccs[0:64, base:base + 64],
          in1=pb[6][0:64, 64 * pr:64 * pr + 64], op=ALU.add)
        O("vector", "tensor_tensor", [sfull[:]], [ccs[:], pb[6].ap()],
          out=sfull[64:128, sb_ + 64:sb_ + 128],
          in0=ccs[64:128, base + 64:base + 128],
          in1=pb[6][64:128, 64 * pr:64 * pr + 64], op=ALU.add)
        O("vector", "tensor_copy", [sfull[:]], [ccs[:]],
          out=sfull[0:64, sb_ + 128:sb_ + 129],
          in_=ccs[0:64, base + 128:base + 129])
        O("vector", "tensor_copy", [sfull[:]], [ccs[:]],
          sem=("sSb" if pr == NPAIR - 1 else None),
          out=sfull[64:128, sb_ + 129:sb_ + 130],
          in_=ccs[64:128, base + 128:base + 129])
    W("tensor", "sSb", p.plan["sSb"])
    free("zall", "zb2a")

    p.barrier()

    free("wq_s", "rbuf", "ebuf", "kqbuf", "vbuf", "phik")
    wo_s = alloc("wo_s", [128, ECH * E], BF16, side="right")
    xch = alloc("xch", [128, 2 * E], F32, side="right")
    xpb = alloc("xpb", [128, 2 * E], F32, side="right")
    ctx = alloc("ctx", [128, 2 * E], BF16, side="right")
    ctxt = alloc("ctxt", [128, 2 * E], BF16, side="right")
    dgb = alloc("dgb", [128, 32], F32, side="right")
    rden = alloc("rden", [128, 32], F32, side="right")
    r0b2 = alloc("r0b2", [128, E], F32, side="right")
    r0p = [r0b[:], r0b2[:]]
    DMA("gpsimd", wo_s[:].rearrange("p (c f) -> p c f", c=ECH),
        woT.rearrange("(c p) f -> p c f", p=128), "qwa")
    wo3 = wo_s[:].rearrange("p (c f) -> p c f", c=ECH)
    sf3 = sfull[:].rearrange("p (r c) -> p r c", r=NPAIR)
    n_ln = float(E)

    num_ck, div_ck, ctxt_ck, ctxe_ck, wo_ck = [], [], [], [], []
    r0_ck, r1_ck, xpb_ck, sq_ck, qx_ck, r1w_ck = [], [], [], [], [], []

    W("tensor", "qwa", p.plan["qwa"])
    for c in range(NCH):
        par = c % 2
        tc8 = 8 * par
        xpc = xpb[:, par * E:(par + 1) * E]
        r0c = r0p[par]
        ctxc = ctx[:, par * E:(par + 1) * E]
        ctxtc = ctxt[:, par * E:(par + 1) * E]
        if c >= 2:
            W("sync", "sXpb", xpb_ck[c - 2])
        qx_ck.append(p.pdma("sync", "qx", xch[:, par * E:(par + 1) * E],
                            x_tm[c * 128:(c + 1) * 128, :]))
        if c > 0:
            W("tensor", "sDiv", div_ck[c - 1])
        for pr in range(NPAIR):
            bk_, co = pr // 2, 256 * (pr % 2)
            lt = pq3[:, pr, c * 128:(c + 1) * 128]
            O("tensor", "ldweights", [], [], weights=lt)
            O("tensor", "matmul", [pb[bk_].ap()], [sfull[:]],
              sem=("sNum" if pr == NPAIR - 1 else None),
              out=pb[bk_][:, co:co + 130], lhsT=lt, rhs=sf3[:, pr, :],
              start=(pr % 2 == 0), stop=True, skip_group_check=True)
        num_ck.append(p.plan["sNum"])
        W("vector", "sNum", num_ck[c])
        numv = pbig.ap()[:, 0:2048].rearrange("p (r f) -> p r f", r=8)
        O("vector", "tensor_scalar", [dgb[:]], [pbig.ap()],
          out=dgb[:, 16 * par:16 * par + 16].rearrange(
              "p (r a) -> p r a", r=8),
          in0=numv[:, :, 128:130], scalar1=EPS_ATTN, scalar2=None,
          op0=ALU.add)
        O("vector", "reciprocal", [rden[:]], [dgb[:]],
          out=rden[:, 16 * par:16 * par + 16],
          in_=dgb[:, 16 * par:16 * par + 16])
        if c >= 2:
            W("vector", "sCtxT", ctxt_ck[c - 2])
        O("vector", "tensor_tensor", [ctx[:]], [pbig.ap(), rden[:]],
          sem="sDiv",
          out=ctxc.rearrange("p (r a b) -> p r a b", r=8, a=2),
          in0=numv[:, :, 0:128].rearrange("p r (a b) -> p r a b", a=2),
          in1=rden[:, 16 * par:16 * par + 16].rearrange(
              "p (r a) -> p r a", r=8).broadcast_to([128, 8, 2, 64]),
          op=ALU.mult)
        div_ck.append(p.plan["sDiv"])
        W("tensor", "sDiv", div_ck[c])
        if c > 0:
            W("tensor", "sCtxE", ctxe_ck[c - 1])
        for j in range(ECH):
            O("tensor", "transpose", [pbb[4]], [ctx[:], eyeb[:]],
              sem=("sCtxT" if j == ECH - 1 else None),
              out=pbb[4][:, 128 * j:128 * (j + 1)],
              in_=ctxc[:, 128 * j:128 * (j + 1)], identity=eyeb[:])
        ctxt_ck.append(p.plan["sCtxT"])
        W("scalar", "sCtxT", ctxt_ck[c])
        if c >= 2:
            W("scalar", "sWo", wo_ck[c - 2])
        O("scalar", "copy", [ctxt[:]], [pbb[4]], sem="sCtxE",
          out=ctxtc, in_=pbb[4][:, :])
        ctxe_ck.append(p.plan["sCtxE"])
        W("tensor", "sCtxE", ctxe_ck[c])
        if c > 0:
            W("tensor", "sR0", r0_ck[c - 1])
        for e in range(ECH):
            lt = ctxtc[:, 128 * e:128 * (e + 1)]
            O("tensor", "ldweights", [], [], weights=lt)
            for half, bnk in ((0, 5), (1, 6)):
                O("tensor", "matmul", [pb[bnk].ap()], [ctxt[:]],
                  sem=("sWo" if (e == ECH - 1 and half == 1) else None),
                  out=pb[bnk][:, :], lhsT=lt,
                  rhs=wo3[:, e, half * 512:(half + 1) * 512],
                  start=(e == 0), stop=(e == ECH - 1))
        wo_ck.append(p.plan["sWo"])
        W("gpsimd", *qx_ck[c])
        if c >= 2:
            W("gpsimd", "sR1", sq_ck[c - 2])
        O("gpsimd", "tensor_tensor", [xpb[:]], [xch[:], bob[:]], sem="sXpb",
          out=xpc, in0=xch[:, par * E:(par + 1) * E], in1=bob[:],
          op=ALU.add)
        xpb_ck.append(p.plan["sXpb"])
        W("vector", "sWo", wo_ck[c])
        W("vector", "sXpb", xpb_ck[c])
        if c >= 2:
            W("vector", "sR1", r1_ck[c - 2])
        O("vector", "memset", [tcol[:]], [], ap=tcol[:, tc8:tc8 + 2],
          constant=0.0)
        O("vector", "memset", [tcol[:]], [], ap=tcol[:, tc8 + 3:tc8 + 4],
          constant=0.0)
        O("vector", "scalar_tensor_tensor", [r0c, tcol[:]],
          [pb[5].ap(), xpb[:]],
          out=r0c[:, 0:512], in0=pb[5][:, :], scalar=1.0, in1=xpc[:, 0:512],
          op0=ALU.mult, op1=ALU.add, accum_out=tcol[:, tc8:tc8 + 1])
        O("vector", "scalar_tensor_tensor", [r0c, tcol[:]],
          [pb[6].ap(), xpb[:]], sem="sR0",
          out=r0c[:, 512:1024], in0=pb[6][:, :], scalar=1.0,
          in1=xpc[:, 512:1024], op0=ALU.mult, op1=ALU.add,
          accum_out=tcol[:, tc8 + 1:tc8 + 2])
        r0_ck.append(p.plan["sR0"])
        O("vector", "tensor_scalar", [tcol[:]], [tcol[:]],
          out=tcol[:, tc8 + 2:tc8 + 3], in0=tcol[:, tc8:tc8 + 1],
          scalar1=tcol[:, tc8 + 1:tc8 + 2], scalar2=None, op0=ALU.add)
        W("scalar", "sR0", r0_ck[c])
        O("scalar", "activation", [xpb[:], tcol[:]], [r0c], sem="sR1",
          out=xpc, in_=r0c, func=AF.Square, accum_out=tcol[:, tc8 + 3:tc8 + 4])
        sq_ck.append(p.plan["sR1"])
        W("vector", "sR1", p.plan["sR1"])
        O("vector", "tensor_scalar", [tcol[:]], [tcol[:]],
          out=tcol[:, tc8 + 4:tc8 + 5], in0=tcol[:, tc8 + 2:tc8 + 3],
          scalar1=tcol[:, tc8 + 2:tc8 + 3], scalar2=-1.0 / (n_ln * n_ln),
          op0=ALU.mult, op1=ALU.mult)
        O("vector", "tensor_scalar", [tcol[:]], [tcol[:]],
          out=tcol[:, tc8 + 4:tc8 + 5], in0=tcol[:, tc8 + 4:tc8 + 5],
          scalar1=EPS_LN, scalar2=None, op0=ALU.add)
        O("vector", "tensor_scalar", [tcol[:]], [tcol[:]], sem="sR1",
          out=tcol[:, tc8 + 5:tc8 + 6], in0=tcol[:, tc8 + 3:tc8 + 4],
          scalar1=1.0 / n_ln, scalar2=tcol[:, tc8 + 4:tc8 + 5],
          op0=ALU.mult, op1=ALU.add)
        W("scalar", "sR1", p.plan["sR1"])
        O("scalar", "activation", [tcol[:]], [tcol[:]], sem="sR1",
          out=tcol[:, tc8 + 6:tc8 + 7], in_=tcol[:, tc8 + 5:tc8 + 6],
          func=AF.Sqrt)
        W("vector", "sR1", p.plan["sR1"])
        O("vector", "reciprocal", [tcol[:]], [tcol[:]],
          out=tcol[:, tc8 + 7:tc8 + 8], in_=tcol[:, tc8 + 6:tc8 + 7])
        O("vector", "tensor_scalar", [tcol[:]], [tcol[:]], sem="sR1",
          out=tcol[:, tc8:tc8 + 1], in0=tcol[:, tc8 + 2:tc8 + 3],
          scalar1=-1.0 / n_ln, scalar2=tcol[:, tc8 + 7:tc8 + 8],
          op0=ALU.mult, op1=ALU.mult)
        if c >= 2:
            W("vector", *r1w_ck[c - 2])
        O("vector", "tensor_scalar", [r1b[:]], [r0c, tcol[:]], sem="sR1",
          out=r1b[:, par * E:(par + 1) * E], in0=r0c,
          scalar1=tcol[:, tc8 + 7:tc8 + 8], scalar2=tcol[:, tc8:tc8 + 1],
          op0=ALU.mult, op1=ALU.add)
        r1_ck.append(p.plan["sR1"])
        W("scalar", "sR1", r1_ck[c])
        r1w_ck.append(p.pdma("scalar", "qr1w", r1h[c * 128:(c + 1) * 128, :],
                             r1b[:, par * E:(par + 1) * E]))

    p.barrier()

    free("r0b2", "rden", "dgb", "ctxt", "ctx", "xpb", "xch", "wo_s")
    free("phiqt", "xqbuf", "wparts", "sfullb", "ccs", "bv2_s",
         "bob_s", "bkb_s")
    wg_s = alloc("wg_s", [128, ECH * INNER], BF16)
    wu_s = alloc("wu_s", [128, ECH * INNER], BF16)
    r1f = alloc("r1f", [128, 2 * E], F32)
    wfst = alloc("wfst2", [128, 4 * 512], F32)
    tern_a = alloc("tern_a2", [128, 2 * 512], BF16)
    tern_b = alloc("tern_b2", [128, 2 * 512], BF16)
    hbuf = alloc("hbuf", [128, INNER], F32, side="right")
    q32h = alloc("q32h", [128, 2 * 512], I32, side="right")
    f1h = alloc("f1h", [128, 2 * 512], F32, side="right")
    rres = alloc("rres", [128, 2 * 512], F32, side="right")
    hqb = alloc("hqb", [128, 2 * 512], BF16, side="right")
    hqt_st = alloc("hqt_st", [128, 2048], BF16, side="right")
    xqt = alloc("xqt", [128, 2 * E], BF16, side="right")
    xt1 = alloc("xt1", [128, 512], F32, side="right")
    silub = alloc("silub", [128, 1024], F32, side="right")
    hmx8 = alloc("hmx8", [128, 16], F32, side="right")

    wg3 = wg_s[:].rearrange("p (c f) -> p c f", c=ECH)
    wu3 = wu_s[:].rearrange("p (c f) -> p c f", c=ECH)
    fpieces = []
    for wi, wd_ in ((0, wgTd), (1, wuTd)):
        r = wd_.rearrange("(c p) f -> p c f", p=128)
        for n in range(NSTR):
            for e in range(ECH):
                fpieces.append((wi, n, e, r[:, e, 512 * n:512 * (n + 1)]))
    tern_ck, ta_ck, wf_ck = [], [], []
    for idx, (wi, n, e, src_) in enumerate(fpieces):
        wsem = "sTg" if wi == 0 else "sTu"
        par_ = idx % 2
        slot = idx % 4
        if idx >= 4:
            W("sync", "sTa", ta_ck[idx - 4])
        stg = wfst[:, slot * 512:(slot + 1) * 512]
        wf_ck.append(p.pdma("sync", "qwf" if idx % 2 == 0 else "qwf2",
                            stg, src_))
        W("scalar", *wf_ck[-1])
        if idx >= 2:
            W("scalar", "sTg" if fpieces[idx - 2][0] == 0 else "sTu",
              tern_ck[idx - 2])
        O("scalar", "activation", [tern_a[:]], [wfst[:], nthb[:], onecol[:]],
          out=tern_a[:, par_ * 512:(par_ + 1) * 512], in_=stg, func=AF.Sign,
          bias=nthb[:, wi:wi + 1], scale=onecol[:, 0:1])
        O("scalar", "activation", [tern_b[:]], [wfst[:], thb[:], onecol[:]],
          sem="sTa",
          out=tern_b[:, par_ * 512:(par_ + 1) * 512], in_=stg, func=AF.Sign,
          bias=thb[:, wi:wi + 1], scale=onecol[:, 0:1])
        ta_ck.append(p.plan["sTa"])
        W("vector", "sTa", ta_ck[-1])
        w_sb3 = wg3 if wi == 0 else wu3
        O("vector", "tensor_tensor", [w_sb3], [tern_a[:], tern_b[:]],
          sem=wsem,
          out=w_sb3[:, e, 512 * n:512 * (n + 1)],
          in0=tern_a[:, par_ * 512:(par_ + 1) * 512],
          in1=tern_b[:, par_ * 512:(par_ + 1) * 512], op=ALU.add)
        tern_ck.append(p.plan[wsem])

    rnd_ck = {"sF1f": [], "sHq": []}
    rnd_n = [0]

    def emit_round(src, s_ap, t1_ap):
        k = rnd_n[0]
        rnd_n[0] += 1
        pi = k % 2
        if k >= 2:
            W("scalar", "sHq", rnd_ck["sHq"][k - 2])
        O("scalar", "activation", [q32h[:]], [src, s_ap],
          out=q32h[:, pi * 512:(pi + 1) * 512], in_=src, func=AF.Copy,
          scale=s_ap, bias=0.0)
        O("scalar", "activation", [f1h[:]], [q32h[:]], sem="sF1f",
          out=f1h[:, pi * 512:(pi + 1) * 512],
          in_=q32h[:, pi * 512:(pi + 1) * 512], func=AF.Copy,
          scale=1.0, bias=-1.0)
        rnd_ck["sF1f"].append(p.plan["sF1f"])
        W("vector", "sF1f", rnd_ck["sF1f"][k])
        O("vector", "scalar_tensor_tensor", [rres[:]], [src, s_ap, f1h[:]],
          out=rres[:, pi * 512:(pi + 1) * 512], in0=src, scalar=s_ap,
          in1=f1h[:, pi * 512:(pi + 1) * 512],
          op0=ALU.mult, op1=ALU.subtract)
        O("vector", "scalar_tensor_tensor", [t1_ap], [rres[:], f1h[:]],
          out=t1_ap, in0=rres[:, pi * 512:(pi + 1) * 512], scalar=1.5,
          in1=f1h[:, pi * 512:(pi + 1) * 512], op0=ALU.is_gt, op1=ALU.add)
        O("vector", "scalar_tensor_tensor",
          [hqb[:]], [rres[:], t1_ap], sem="sHq",
          out=hqb[:, pi * 512:(pi + 1) * 512],
          in0=rres[:, pi * 512:(pi + 1) * 512], scalar=0.5,
          in1=t1_ap, op0=ALU.is_ge, op1=ALU.add)
        rnd_ck["sHq"].append(p.plan["sHq"])
        return k

    r1free_ck, xqt_ck, xqe_ck, gu_last_ck, r1r_ck = [], [], [], [], []
    silu_ck, h_ck, hqe_ck, hw_ck = [], [], [], []
    hqbfree = []

    def emit_hquant_eighth(c, g):
        hs = hbuf[:, 512 * g:512 * (g + 1)]
        fc2 = 8 * (c % 2)
        ku = emit_round(hs, fcol[:, fc2 + 7:fc2 + 8], hs)
        if ku >= 2:
            W("vector", *hqbfree[ku - 2])
        W("tensor", "sHq", rnd_ck["sHq"][ku])
        quarter = g // 2
        bnk = 5 + (quarter % 2)
        gco = (g % 2) * 512
        if g % 2 == 0 and len(hqe_ck) >= 2:
            W("tensor", "sHqE", hqe_ck[-2])
        for j in range(4):
            O("tensor", "transpose", [pbb[bnk]], [hqb[:], eyeb[:]],
              sem=("sHqT" if j == 3 else None),
              out=pbb[bnk][:, gco + 128 * j:gco + 128 * (j + 1)],
              in_=hqb[:, (ku % 2) * 512 + 128 * j:
                      (ku % 2) * 512 + 128 * (j + 1)], identity=eyeb[:])
        hqbfree.append(("sHqT", p.plan["sHqT"]))
        if g % 2 == 1:
            qhalf = quarter % 2
            G = 4 * c + quarter
            if G // 2 >= 1:
                W("vector", *hw_ck[G // 2 - 1])
            W("vector", "sHqT", p.plan["sHqT"])
            O("vector", "tensor_copy", [hqt_st[:]], [pbb[bnk]],
              sem="sHqE",
              out=hqt_st[:, 1024 * qhalf:1024 * (qhalf + 1)],
              in_=pbb[bnk][:, :])
            hqe_ck.append(p.plan["sHqE"])
            if quarter % 2 == 1:
                half = quarter // 2
                W("scalar", "sHqE", hqe_ck[-1])
                hw_ck.append(p.pdma(
                    "scalar", "qhw",
                    hqTh[c, :, 2048 * half:2048 * (half + 1)],
                    hqt_st[:]))
        return rnd_ck["sHq"][ku]

    def emit_r1_load(cc):
        if cc >= 2:
            W("sync", "sHq", r1free_ck[cc - 2])
        W("sync", *r1w_ck[cc])
        r1r_ck.append(p.pdma("sync", "qr1r",
                             r1f[:, (cc % 2) * E:(cc % 2 + 1) * E],
                             r1h[cc * 128:(cc + 1) * 128, :]))

    def emit_head(cc):
        xp = cc % 2
        fc = 8 * xp
        r1c = r1f[:, xp * E:(xp + 1) * E]
        W("vector", *r1r_ck[cc])
        if cc >= 2:
            W("vector", "sSilu", silu_ck[8 * (cc - 2) + 7])
        O("vector", "tensor_reduce", [fcol[:]], [r1f[:]],
          out=fcol[:, fc:fc + 1], in_=r1c, axis=AXL.X, op=ALU.max,
          apply_absolute_value=True)
        O("vector", "tensor_scalar", [fcol[:]], [fcol[:]],
          out=fcol[:, fc + 1:fc + 2], in0=fcol[:, fc:fc + 1], scalar1=1e-5,
          scalar2=None, op0=ALU.max)
        O("vector", "reciprocal", [fcol[:]], [fcol[:]],
          out=fcol[:, fc + 2:fc + 3], in_=fcol[:, fc + 1:fc + 2])
        O("vector", "tensor_scalar", [fcol[:]], [fcol[:]], sem="sXq",
          out=fcol[:, fc + 3:fc + 4], in0=fcol[:, fc + 2:fc + 3],
          scalar1=127.0, scalar2=None, op0=ALU.mult)
        O("vector", "tensor_scalar", [fcol[:]], [fcol[:], mh127[:]],
          out=fcol[:, fc + 4:fc + 5], in0=fcol[:, fc + 1:fc + 2],
          scalar1=mh127[:, 0:1], scalar2=None, op0=ALU.mult)
        O("vector", "tensor_scalar", [fcol[:]], [fcol[:], mh127[:]],
          out=fcol[:, fc + 5:fc + 6], in0=fcol[:, fc + 1:fc + 2],
          scalar1=mh127[:, 1:2], scalar2=None, op0=ALU.mult)
        if cc > 0:
            W("tensor", "sXqE", xqe_ck[cc - 1])
        W("scalar", "sXq", p.plan["sXq"])
        for xh in range(2):
            ku = emit_round(r1c[:, 512 * xh:512 * (xh + 1)],
                            fcol[:, fc + 3:fc + 4], xt1[:])
            if ku >= 2:
                W("vector", *hqbfree[ku - 2])
            W("tensor", "sHq", rnd_ck["sHq"][ku])
            for j in range(4):
                O("tensor", "transpose", [pbb[0]], [hqb[:], eyeb[:]],
                  sem=("sXqT" if j == 3 else None),
                  out=pbb[0][:, 512 * xh + 128 * j:
                             512 * xh + 128 * (j + 1)],
                  in_=hqb[:, (ku % 2) * 512 + 128 * j:
                          (ku % 2) * 512 + 128 * (j + 1)],
                  identity=eyeb[:])
            xqt_ck.append(p.plan["sXqT"])
            hqbfree.append(("sXqT", p.plan["sXqT"]))
        r1free_ck.append(rnd_ck["sHq"][-1])
        W("vector", "sXqT", xqt_ck[-1])
        if cc >= 2:
            W("vector", "sGU", gu_last_ck[cc - 2])
        O("vector", "tensor_copy", [xqt[:]], [pbb[0]], sem="sXqE",
          out=xqt[:, xp * E:(xp + 1) * E], in_=pbb[0][:, :])
        xqe_ck.append(p.plan["sXqE"])

    emit_r1_load(0)
    emit_head(0)
    for c in range(NCH + 1):
        par = c % 2
        if c + 1 < NCH:
            emit_r1_load(c + 1)
        if c > 0:
            hp = (c - 1) % 2
            fc2 = 8 * hp
            if rnd_ck["sF1f"]:
                W("vector", "sF1f", rnd_ck["sF1f"][-1])
            O("vector", "tensor_reduce", [fcol[:]], [hmx8[:]],
              out=fcol[:, fc2 + 6:fc2 + 7], in_=hmx8[:, 8 * hp:8 * hp + 8],
              axis=AXL.X, op=ALU.max)
            O("vector", "tensor_scalar", [mxh_all[:]], [fcol[:]],
              out=mxh_all[:, c - 1:c], in0=fcol[:, fc2 + 6:fc2 + 7],
              scalar1=1e-5, scalar2=None, op0=ALU.max)
            O("vector", "reciprocal", [fcol[:]], [mxh_all[:]],
              out=fcol[:, fc2 + 7:fc2 + 8], in_=mxh_all[:, c - 1:c])
            O("vector", "tensor_scalar", [fcol[:]], [fcol[:]], sem="sXq",
              out=fcol[:, fc2 + 7:fc2 + 8], in0=fcol[:, fc2 + 7:fc2 + 8],
              scalar1=127.0, scalar2=None, op0=ALU.mult)
            W("scalar", "sXq", p.plan["sXq"])
        if c < NCH:
            W("tensor", "sXqE", xqe_ck[c])
        for n in range(NSTR):
            if c < NCH:
                gb, ub = (1, 2) if n % 2 == 0 else (3, 4)
                if c == 0:
                    W("tensor", "sTg", 8 * (n + 1))
                    W("tensor", "sTu", 8 * (n + 1))
                hidx = c * NSTR + n
                if hidx >= 2:
                    W("tensor", "sH", h_ck[hidx - 2])
                    W("tensor", "sSilu", silu_ck[hidx - 2])
                for e in range(ECH):
                    lt = xqt[:, par * E + 128 * e:par * E + 128 * (e + 1)]
                    O("tensor", "ldweights", [], [], weights=lt)
                    O("tensor", "matmul", [pb[gb].ap()], [xqt[:], wg_s[:]],
                      out=pb[gb][:, :], lhsT=lt,
                      rhs=wg3[:, e, 512 * n:512 * (n + 1)],
                      start=(e == 0), stop=(e == ECH - 1))
                    O("tensor", "matmul", [pb[ub].ap()], [xqt[:], wu_s[:]],
                      sem=("sGU" if e == ECH - 1 else None),
                      out=pb[ub][:, :], lhsT=lt,
                      rhs=wu3[:, e, 512 * n:512 * (n + 1)],
                      start=(e == 0), stop=(e == ECH - 1))
            if c > 0:
                emit_hquant_eighth(c - 1, n)
            if n == 0 and c + 1 < NCH:
                emit_head(c + 1)
            if c < NCH:
                spar = n % 2
                fc = 8 * par
                hidx = c * NSTR + n
                W("scalar", "sGU", p.plan["sGU"])
                if hidx >= 2:
                    W("scalar", "sH", h_ck[hidx - 2])
                O("scalar", "activation", [silub[:]],
                  [pb[gb].ap(), fcol[:], zcol[:]], sem="sSilu",
                  out=silub[:, spar * 512:(spar + 1) * 512], in_=pb[gb][:, :],
                  func=AF.Silu, scale=fcol[:, fc + 4:fc + 5], bias=zcol[:, 0:1])
                silu_ck.append(p.plan["sSilu"])
                W("vector", "sSilu", silu_ck[-1])
                O("vector", "scalar_tensor_tensor", [hbuf[:]],
                  [pb[ub].ap(), fcol[:], silub[:]],
                  out=hbuf[:, 512 * n:512 * (n + 1)], in0=pb[ub][:, :],
                  scalar=fcol[:, fc + 5:fc + 6],
                  in1=silub[:, spar * 512:(spar + 1) * 512],
                  op0=ALU.mult, op1=ALU.mult)
                O("vector", "tensor_reduce", [hmx8[:]], [hbuf[:]], sem="sH",
                  out=hmx8[:, 8 * par + n:8 * par + n + 1],
                  in_=hbuf[:, 512 * n:512 * (n + 1)],
                  axis=AXL.X, op=ALU.max, apply_absolute_value=True)
                h_ck.append(p.plan["sH"])
        if c < NCH:
            gu_last_ck.append(p.plan["sGU"])

    p.barrier()

    free("hmx8", "silub", "xt1", "xqt", "hqt_st", "hqb", "rres", "f1h",
         "q32h", "hbuf")
    wd3 = wg_s[:].rearrange("p (c f) -> p c f", c=ICH)
    hqL = wu_s
    rdw = wdTd.rearrange("(c p) f -> p c f", p=128)
    td_ck = []
    ta2_ck = []
    for i in range(ICH):
        for hh in range(2):
            idx = 2 * i + hh
            par_ = idx % 2
            slot = idx % 4
            if idx >= 4:
                W("sync", "sTa", ta2_ck[idx - 4])
            stg = wfst[:, slot * 512:(slot + 1) * 512]
            wf_ck.append(p.pdma("sync", "qwf" if idx % 2 == 0 else "qwf2",
                                stg,
                                rdw[:, i, 512 * hh:512 * (hh + 1)]))
            W("scalar", *wf_ck[-1])
            if idx >= 2:
                W("scalar", "sTd", td_ck[idx - 2])
            O("scalar", "activation", [tern_a[:]], [wfst[:], nthb[:], onecol[:]],
              out=tern_a[:, par_ * 512:(par_ + 1) * 512], in_=stg,
              func=AF.Sign, bias=nthb[:, 2:3], scale=onecol[:, 0:1])
            O("scalar", "activation", [tern_b[:]], [wfst[:], thb[:], onecol[:]],
              sem="sTa",
              out=tern_b[:, par_ * 512:(par_ + 1) * 512], in_=stg,
              func=AF.Sign, bias=thb[:, 2:3], scale=onecol[:, 0:1])
            ta2_ck.append(p.plan["sTa"])
            W("vector", "sTa", ta2_ck[-1])
            O("vector", "tensor_tensor", [wg_s[:]], [tern_a[:], tern_b[:]],
              sem="sTd",
              out=wd3[:, i, 512 * hh:512 * (hh + 1)],
              in0=tern_a[:, par_ * 512:(par_ + 1) * 512],
              in1=tern_b[:, par_ * 512:(par_ + 1) * 512], op=ALU.add)
            td_ck.append(p.plan["sTd"])

    dn_ck, ep_ck, epr_ck, hr_ck, r1r2_ck, qo_ck = [], [], [], [], [], []
    W("sync", "qhw0", p.plan["qhw0"])
    W("sync", "qhw1", p.plan["qhw1"])
    W("sync", "qr1w0", p.plan["qr1w0"])
    W("sync", "qr1w1", p.plan["qr1w1"])
    for c in range(NCH):
        par = c % 2
        hqc = hqL[:, par * INNER:(par + 1) * INNER]
        if c >= 2:
            W("sync", "sDn", dn_ck[c - 2])
        hr_ck.append(p.pdma("sync", "qhr", hqc, hqTh[c, :, :]))
        if c >= 2:
            W("sync", "sEp", epr_ck[c - 2])
        r1r2_ck.append(p.pdma("sync", "qr1r",
                              r1f[:, par * E:(par + 1) * E],
                              r1h[c * 128:(c + 1) * 128, :]))
        W("tensor", *hr_ck[c])
        db0, db1 = (1, 2) if par == 0 else (3, 4)
        if c > 1:
            W("tensor", "sEp", epr_ck[c - 2])
        for i in range(ICH):
            if c == 0:
                W("tensor", "sTd", td_ck[2 * i + 1])
            lt = hqc[:, 128 * i:128 * (i + 1)]
            O("tensor", "ldweights", [], [], weights=lt)
            O("tensor", "matmul", [pb[db0].ap()], [hqL[:], wg_s[:]],
              out=pb[db0][:, :], lhsT=lt, rhs=wd3[:, i, 0:512],
              start=(i == 0), stop=(i == ICH - 1))
            O("tensor", "matmul", [pb[db1].ap()], [hqL[:], wg_s[:]],
              sem=("sDn" if i == ICH - 1 else None),
              out=pb[db1][:, :], lhsT=lt, rhs=wd3[:, i, 512:1024],
              start=(i == 0), stop=(i == ICH - 1))
        dn_ck.append(p.plan["sDn"])
        r1c = r1f[:, par * E:(par + 1) * E]
        W("vector", *r1r2_ck[c])
        W("vector", "sDn", dn_ck[c])
        if c > 0:
            W("vector", "sEp", ep_ck[c - 1])
        O("vector", "memset", [fcol[:]], [], ap=fcol[:, 1:3], constant=0.0)
        O("vector", "memset", [fcol[:]], [], ap=fcol[:, 4:5], constant=0.0)
        O("vector", "tensor_scalar", [fcol[:]], [mxh_all[:], mh127[:]],
          out=fcol[:, 0:1], in0=mxh_all[:, c:c + 1], scalar1=mh127[:, 2:3],
          scalar2=None, op0=ALU.mult)
        O("vector", "scalar_tensor_tensor", [r0b[:], fcol[:]],
          [pb[db0].ap(), r1f[:]],
          out=r0b[:, 0:512], in0=pb[db0][:, :], scalar=fcol[:, 0:1],
          in1=r1c[:, 0:512], op0=ALU.mult, op1=ALU.add,
          accum_out=fcol[:, 1:2])
        O("vector", "scalar_tensor_tensor", [r0b[:], fcol[:]],
          [pb[db1].ap(), r1f[:]], sem="sEp",
          out=r0b[:, 512:1024], in0=pb[db1][:, :], scalar=fcol[:, 0:1],
          in1=r1c[:, 512:1024], op0=ALU.mult, op1=ALU.add,
          accum_out=fcol[:, 2:3])
        epr_ck.append(p.plan["sEp"])
        O("vector", "tensor_scalar", [fcol[:]], [fcol[:]],
          out=fcol[:, 3:4], in0=fcol[:, 1:2], scalar1=fcol[:, 2:3],
          scalar2=None, op0=ALU.add)
        W("scalar", "sEp", p.plan["sEp"])
        O("scalar", "activation", [wfst[:], fcol[:]], [r0b[:]], sem="sEp",
          out=wfst[:, 0:1024], in_=r0b[:], func=AF.Square,
          accum_out=fcol[:, 4:5])
        W("vector", "sEp", p.plan["sEp"])
        O("vector", "tensor_scalar", [fcol[:]], [fcol[:]],
          out=fcol[:, 5:6], in0=fcol[:, 3:4], scalar1=fcol[:, 3:4],
          scalar2=-1.0 / (n_ln * n_ln), op0=ALU.mult, op1=ALU.mult)
        O("vector", "tensor_scalar", [fcol[:]], [fcol[:]],
          out=fcol[:, 5:6], in0=fcol[:, 5:6], scalar1=EPS_LN, scalar2=None,
          op0=ALU.add)
        O("vector", "tensor_scalar", [fcol[:]], [fcol[:]], sem="sEp",
          out=fcol[:, 6:7], in0=fcol[:, 4:5], scalar1=1.0 / n_ln,
          scalar2=fcol[:, 5:6], op0=ALU.mult, op1=ALU.add)
        W("scalar", "sEp", p.plan["sEp"])
        O("scalar", "activation", [fcol[:]], [fcol[:]], sem="sEp",
          out=fcol[:, 6:7], in_=fcol[:, 6:7], func=AF.Sqrt)
        W("vector", "sEp", p.plan["sEp"])
        O("vector", "reciprocal", [fcol[:]], [fcol[:]],
          out=fcol[:, 7:8], in_=fcol[:, 6:7])
        O("vector", "tensor_scalar", [fcol[:]], [fcol[:]], sem="sEp",
          out=fcol[:, 5:6], in0=fcol[:, 3:4], scalar1=-1.0 / n_ln,
          scalar2=fcol[:, 7:8], op0=ALU.mult, op1=ALU.mult)
        if c >= 2:
            W("vector", *qo_ck[c - 2])
        O("vector", "tensor_scalar", [r1b[:]], [r0b[:], fcol[:]], sem="sEp",
          out=r1b[:, par * E:(par + 1) * E], in0=r0b[:],
          scalar1=fcol[:, 7:8], scalar2=fcol[:, 5:6],
          op0=ALU.mult, op1=ALU.add)
        ep_ck.append(p.plan["sEp"])
        W("scalar", "sEp", ep_ck[c])
        qo_ck.append(p.pdma("scalar", "qo", out_d[c * 128:(c + 1) * 128, :],
                            r1b[:, par * E:(par + 1) * E]))

    if DBG:
        W("sync", "qo0", p.plan["qo0"])
        W("sync", "qo1", p.plan["qo1"])
        DMA("sync", dbg_r1, r1h, "qcc")
        DMA("sync", dbg_sf, cc_out.ap(), "qcc")
        DMA("sync", dbg_mh, mhb[:], "qcc")
        DMA("sync", dbg_hq, hqTh[0, :, :], "qcc")
        W("sync", "qcc", p.plan["qcc"])
    W("sync", "qo0", p.plan["qo0"])
    W("sync", "qo1", p.plan["qo1"])

    with nc.Block() as block:
        @block.sync
        def _(eng):
            for f in p.q["sync"]:
                f(eng)

        @block.scalar
        def _(eng):
            for f in p.q["scalar"]:
                f(eng)

        @block.vector
        def _(eng):
            for f in p.q["vector"]:
                f(eng)

        @block.tensor
        def _(eng):
            for f in p.q["tensor"]:
                f(eng)

        @block.gpsimd
        def _(eng):
            for f in p.q["gpsimd"]:
                f(eng)
    return nc, p


def kernel(L0_attn_query, L0_attn_Wq, L0_attn_bq, L0_attn_Wk, L0_attn_bk,
           L0_attn_Wv, L0_attn_bv, L0_attn_Wo, L0_attn_bo,
           L0_ffn_w_gate, L0_ffn_w_up, L0_ffn_w_down):
    x = np.ascontiguousarray(
        np.asarray(L0_attn_query, dtype=np.float32).reshape(B * L, E))
    asf = lambda a: np.ascontiguousarray(np.asarray(a, dtype=np.float32))
    Wq, Wk, Wv, Wo = map(asf, (L0_attn_Wq, L0_attn_Wk, L0_attn_Wv, L0_attn_Wo))
    bq, bk, bv, bo = map(asf, (L0_attn_bq, L0_attn_bk, L0_attn_bv, L0_attn_bo))
    wg, wu, wd = map(asf, (L0_ffn_w_gate, L0_ffn_w_up, L0_ffn_w_down))

    common = {
        "wqT": np.ascontiguousarray(Wq.T), "wkT": np.ascontiguousarray(Wk.T),
        "wvT": np.ascontiguousarray(Wv.T), "woT": np.ascontiguousarray(Wo.T),
        "bqpp": np.ascontiguousarray(bq.reshape(ECH, 128).T),
        "bkb": np.ascontiguousarray(np.broadcast_to(bk, (128, E))),
        "bob": np.ascontiguousarray(np.broadcast_to(bo, (128, E))),
        "bv2": np.ascontiguousarray(
            bv.reshape(NPAIR, 2, D).transpose(1, 0, 2).reshape(2, 512)),
        "wgT": np.ascontiguousarray(wg.T), "wuT": np.ascontiguousarray(wu.T),
        "wdT": np.ascontiguousarray(wd.T),
        "eyeb": np.eye(128, dtype=ml_dtypes.bfloat16),
        "eyef": np.eye(128, dtype=np.float32),
    }
    in_maps = []
    for c in range(NCORES):
        xc = x[c * TPC:(c + 1) * TPC, :]
        m = dict(common)
        m["x"] = np.ascontiguousarray(xc)
        m["xT"] = np.ascontiguousarray(xc.T)
        in_maps.append(m)

    nc, _ = build_full()
    res = run_bass_kernel_spmd(nc, in_maps, core_ids=list(range(NCORES)))
    outs = [np.asarray(res.results[c]["out"], dtype=np.float32)
            for c in range(NCORES)]
    return np.concatenate(outs, axis=0).reshape(B, L, E)
